# version 04
# baseline (speedup 1.0000x reference)
import numpy as np
import contextlib
import concourse.bass as bass
import concourse.mybir as mybir
from concourse.bass_utils import run_bass_kernel_spmd

F32 = mybir.dt.float32
BF16 = mybir.dt.bfloat16
AF = mybir.ActivationFunctionType
ALU = mybir.AluOpType
AX = mybir.AxisListType

TOK = 2048
EMB = 4096
ECH = 32
DFF = 11008
FCH = 86
TILE = 256
NTILE = TOK // TILE
BLK = 128
NBLK = TOK // BLK
GRP = 4
NGRP = TILE // GRP
TG3 = 4
SUB = 4
DMCH = 32
INV_SQRT_D = 0.08838834764831845
INV_EMB = 1.0 / 4096.0
LN_EPS = 1e-5


def _th(x):
    return x.tensor if hasattr(x, "tensor") else x


class _Ctr:
    __slots__ = ("h", "n")

    def __init__(self, h):
        self.h = h
        self.n = 0


def build_nc():
    nc = bass.Bass("TRN2", target_bir_lowering=False)

    x_in = nc.dram_tensor("x", [TOK, EMB], F32, kind="ExternalInput")
    wq_r = nc.dram_tensor("wq_r", [EMB, 4096], F32, kind="ExternalInput")
    wk_r = nc.dram_tensor("wk_r", [EMB, 1024], F32, kind="ExternalInput")
    wv_r = nc.dram_tensor("wv_r", [EMB, 1024], F32, kind="ExternalInput")
    bq_in = nc.dram_tensor("bq", [128, 32], F32, kind="ExternalInput")
    bk_in = nc.dram_tensor("bk", [128, 8], F32, kind="ExternalInput")
    bv_in = nc.dram_tensor("bv", [128, 8], F32, kind="ExternalInput")
    wupT = nc.dram_tensor("wupT", [EMB, DFF], F32, kind="ExternalInput")
    wgT = nc.dram_tensor("wgT", [EMB, DFF], F32, kind="ExternalInput")
    wdT = nc.dram_tensor("wdT", [DFF, EMB], F32, kind="ExternalInput")
    idn = nc.dram_tensor("idn", [128, 128], F32, kind="ExternalInput")
    out_h = nc.dram_tensor("out", [TOK, EMB], F32, kind="ExternalOutput")

    wqg_bf = nc.dram_tensor("wqg_bf", [EMB, 1024], BF16, kind="Internal")
    wd_bf = nc.dram_tensor("wd_bf", [DFF, EMB], BF16, kind="Internal")
    xn_sp = nc.dram_tensor("xn_sp", [EMB, TOK], BF16, kind="Internal")
    attn_sp = nc.dram_tensor("attn_sp", [EMB, TOK], F32, kind="Internal")
    h_sp = nc.dram_tensor("h_sp", [DFF, TOK], BF16, kind="Internal")
    y_sp = nc.dram_tensor("y_sp", [EMB, TOK], F32, kind="Internal")

    g = contextlib.ExitStack()
    with g:
        es = g.enter_context
        idb = es(nc.sbuf_tensor("idb", [128, 128], BF16))
        idf = es(nc.sbuf_tensor("idf", [128, 128], F32))
        bqs = es(nc.sbuf_tensor("bqs", [128, 32], F32))
        bks = es(nc.sbuf_tensor("bks", [128, 8], F32))
        bvs = es(nc.sbuf_tensor("bvs", [128, 8], F32))
        bqg = es(nc.sbuf_tensor("bqg", [128, 8], F32))
        bt1 = es(nc.sbuf_tensor("bt1", [128, 8], F32))
        bt2 = es(nc.sbuf_tensor("bt2", [128, 8], F32))

        names = ["DV", "AC", "PE", "GM",
                 "SX", "SW", "SXN", "GKV", "GSP",
                 "SQS", "SQW", "SIN",
                 "GUP", "SH", "GWD", "SWD",
                 "SH3", "SWB", "SY",
                 "SCL", "GOUT"]
        S = {n: _Ctr(es(nc.semaphore(n))) for n in names}
        DV, ACS, PE, GM = S["DV"], S["AC"], S["PE"], S["GM"]

        block = es(nc.Block())

        def d_op(v, fn, dep=None):
            if dep is not None:
                v.wait_ge(DV.h, dep)
            fn().then_inc(DV.h, 1)
            DV.n += 1
            return DV.n

        def a_op(s, fn):
            s.wait_ge(ACS.h, ACS.n)
            fn().then_inc(ACS.h, 1)
            ACS.n += 1
            return ACS.n

        def p_mark(inst):
            inst.then_inc(PE.h, 1)
            PE.n += 1
            return PE.n

        def dma(q, sem, out, in_):
            q.dma_start(out=out, in_=in_).then_inc(sem.h, 16)
            sem.n += 16
            return sem.n

        def barrier():
            cnts = [(c.h, c.n) for c in S.values()]
            for engname in ("tensor", "vector", "scalar", "sync", "gpsimd"):
                def _mk(e):
                    for h, n in cnts:
                        e.wait_ge(h, n)
                getattr(block, engname)(_mk)

        tr_state = {"idx": 0, "ev_dv": [], "tr_pe": []}

        def emit_transpose(t, src, psums):
            n = tr_state["idx"]
            bank = psums[n % 2]
            slot = (n // 2) % 4
            view = bank[:, slot * 128:(slot + 1) * 128]
            if n >= 2:
                t.wait_ge(DV.h, tr_state["ev_dv"][n - 2])
            inst = t.transpose(view, src, idb[:])
            p_mark(inst)
            tr_state["tr_pe"].append(PE.n)
            tr_state["idx"] += 1
            return view, n

        with contextlib.ExitStack() as sc0:
            e0 = sc0.enter_context
            WQS = [e0(nc.sbuf_tensor(f"WQS{i}", [128, 4096], F32)) for i in range(2)]
            s01 = e0(nc.sbuf_tensor("s01", [128, 1024], F32))
            s23 = e0(nc.sbuf_tensor("s23", [128, 1024], F32))
            ssum = e0(nc.sbuf_tensor("ssum", [128, 1024], F32))
            wqg_t = [e0(nc.sbuf_tensor(f"wqgt{i}", [128, 1024], BF16)) for i in range(2)]

            def _gp(gq):
                dma(gq, S["SIN"], idb[:], idn.ap())
            block.gpsimd(_gp)

            def _sy(sq):
                dma(sq, S["SIN"], idf[:], idn.ap())
                dma(sq, S["SIN"], bqs[:], bq_in.ap())
                dma(sq, S["SIN"], bks[:], bk_in.ap())
                dma(sq, S["SIN"], bvs[:], bv_in.ap())
            block.sync(_sy)

            def _dv(v):
                v.wait_ge(S["SIN"].h, S["SIN"].n)
                bqh = _th(bqs)
                views = [bass.AP(bqh, j, [[32, 128], [4, 8]]) for j in range(4)]
                d_op(v, lambda: v.tensor_tensor(bt1[:], views[0], views[1], ALU.add))
                d_op(v, lambda: v.tensor_tensor(bt2[:], views[2], views[3], ALU.add))
                d_op(v, lambda: v.tensor_tensor(bt1[:], bt1[:], bt2[:], ALU.add), dep=DV.n)
                d_op(v, lambda: v.tensor_scalar(bqg[:], bt1[:], INV_SQRT_D, None, ALU.mult), dep=DV.n)
            block.vector(_dv)

            s23_dv = [0] * ECH
            st_sq = [0] * ECH
            for ec in range(ECH):
                buf = ec % 2

                def _ld(sq, ec=ec, buf=buf):
                    if ec >= 2:
                        sq.wait_ge(DV.h, s23_dv[ec - 2])
                    src = bass.AP(_th(wq_r), ec * 128 * 4096, [[4096, 128], [1, 4096]])
                    dma(sq, S["SQS"], WQS[buf][:], src)
                block.sync(_ld)

                def _gs(v, ec=ec, buf=buf):
                    v.wait_ge(S["SQS"].h, (ec + 1) * 16)
                    if ec >= 2:
                        v.wait_ge(S["SQW"].h, st_sq[ec - 2])
                    wh = _th(WQS[buf])
                    vw = [bass.AP(wh, j * 128, [[4096, 128], [512, 8], [1, 128]]) for j in range(4)]
                    d_op(v, lambda: v.tensor_tensor(s01[:], vw[0], vw[1], ALU.add))
                    d_op(v, lambda: v.tensor_tensor(s23[:], vw[2], vw[3], ALU.add))
                    s23_dv[ec] = DV.n
                    d_op(v, lambda: v.tensor_tensor(ssum[:], s01[:], s23[:], ALU.add), dep=DV.n)
                    d_op(v, lambda: v.tensor_scalar(wqg_t[buf][:], ssum[:], INV_SQRT_D, None, ALU.mult), dep=DV.n)
                block.vector(_gs)

                def _st(sq, ec=ec, buf=buf):
                    sq.wait_ge(DV.h, DV.n)
                    dst = bass.AP(_th(wqg_bf), ec * 128 * 1024, [[1024, 128], [1, 1024]])
                    st_sq[ec] = dma(sq, S["SQW"], dst, wqg_t[buf][:])
                block.sync(_st)

        barrier()

        with contextlib.ExitStack() as scA:
            eA = scA.enter_context
            XB = [eA(nc.sbuf_tensor(f"XB{i}", [128, EMB], F32)) for i in range(2)]
            XSQ = eA(nc.sbuf_tensor("XSQ", [128, EMB], F32))
            XNT = [eA(nc.sbuf_tensor(f"XNT{i}", [128, EMB], BF16)) for i in range(2)]
            XNF = eA(nc.sbuf_tensor("XNF", [128, ECH * TILE], BF16))
            WB = [eA(nc.sbuf_tensor(f"WB{i}", [128, 4096], BF16)) for i in range(3)]
            QG32 = eA(nc.sbuf_tensor("QG32", [128, TILE * 32], BF16))
            K32 = eA(nc.sbuf_tensor("K32", [128, TILE * 32], BF16))
            VA = eA(nc.sbuf_tensor("VA", [128, TILE * 9], BF16))
            QGT = [eA(nc.sbuf_tensor(f"QGT{i}", [128, 128], BF16)) for i in range(2)]
            KTT = [eA(nc.sbuf_tensor(f"KTT{i}", [128, 128], BF16)) for i in range(2)]
            ET = [eA(nc.sbuf_tensor(f"ET{i}", [128, 512], BF16)) for i in range(2)]
            O8N = [eA(nc.sbuf_tensor(f"O8N{i}", [128, 8 * TILE], F32)) for i in range(2)]
            REC = [eA(nc.sbuf_tensor(f"REC{i}", [128, 4], F32)) for i in range(2)]
            st1 = eA(nc.sbuf_tensor("st1", [128, 1], F32))
            st2 = eA(nc.sbuf_tensor("st2", [128, 1], F32))
            stm = eA(nc.sbuf_tensor("stm", [128, 1], F32))
            stv = eA(nc.sbuf_tensor("stv", [128, 1], F32))
            str_ = eA(nc.sbuf_tensor("str", [128, 1], F32))
            stl = eA(nc.sbuf_tensor("stl", [128, 1], F32))
            stn = eA(nc.sbuf_tensor("stn", [128, 1], F32))
            pQKV = [eA(nc.psum_tensor(f"pQKV{i}", [128, 512], F32)) for i in range(2)]
            pTR = [eA(nc.psum_tensor(f"pTR{i}", [128, 512], BF16)) for i in range(2)]
            pSC = eA(nc.psum_tensor("pSC", [128, 2048], F32))
            pSCh = _th(pSC)

            def _ini(gq):
                gq.memset(QG32[:], 0.0).then_inc(GM.h, 1); GM.n += 1
                gq.memset(K32[:], 0.0).then_inc(GM.h, 1); GM.n += 1
                gq.memset(bass.AP(_th(VA), 8, [[TILE * 9, 128], [9, TILE]]), 1.0).then_inc(GM.h, 1)
                GM.n += 1
            block.gpsimd(_ini)
            gm_ready = GM.n

            norm_dv = [0, 0]
            trlast_pe = [0, 0]
            xnf_spill_sx = 0
            qkv_last_pe = 0
            attn_av_pe = 0
            o8_spill_gs = [0, 0]
            wld = []
            wmm_pe = []
            evac_dv = []

            def w_chunk_src(ci):
                m = ci % 8
                if ci < 8:
                    return "sync", S["SW"], bass.AP(_th(wqg_bf), m * 128,
                                                    [[1024, 128], [128 * 1024, 32], [1, 128]])
                src = wk_r if ci < 16 else wv_r
                return "gpsimd", S["GKV"], bass.AP(_th(src), m * 128,
                                                   [[1024, 128], [128 * 1024, 32], [1, 128]])

            def emit_w_load(ti, ci):
                gi = ti * 24 + ci
                eng, sem, src = w_chunk_src(ci)

                def _w(q, gi=gi, sem=sem, src=src):
                    if gi >= 3:
                        q.wait_ge(PE.h, wmm_pe[gi - 3])
                    wld.append((sem, dma(q, sem, WB[gi % 3][:], src)))
                getattr(block, eng)(_w)

            for ti in range(NTILE):
                def _xl(sq, ti=ti):
                    for b in range(2):
                        gb = ti * 2 + b
                        if gb >= 2:
                            sq.wait_ge(DV.h, norm_dv[gb % 2])
                        src = bass.AP(_th(x_in), gb * 128 * EMB, [[EMB, 128], [1, EMB]])
                        dma(sq, S["SX"], XB[gb % 2][:], src)
                block.sync(_xl)
                if ti == 0:
                    for ci in range(3):
                        emit_w_load(0, ci)

                for b in range(2):
                    gb = ti * 2 + b

                    def _st(v, gb=gb):
                        v.wait_ge(S["SX"].h, (gb + 1) * 16)
                        xb = XB[gb % 2]
                        d_op(v, lambda: v.reduce_sum(st1[:], xb[:], AX.X))
                        d_op(v, lambda: v.tensor_tensor(XSQ[:], xb[:], xb[:], ALU.mult))
                        d_op(v, lambda: v.reduce_sum(st2[:], XSQ[:], AX.X), dep=DV.n)
                        d_op(v, lambda: v.tensor_scalar(stm[:], st1[:], INV_EMB, None, ALU.mult), dep=DV.n)
                        d_op(v, lambda: v.tensor_scalar(st2[:], st2[:], INV_EMB, None, ALU.mult), dep=DV.n)
                        d_op(v, lambda: v.tensor_tensor(st1[:], stm[:], stm[:], ALU.mult), dep=DV.n)
                        d_op(v, lambda: v.tensor_tensor(stv[:], st2[:], st1[:], ALU.subtract), dep=DV.n)
                        d_op(v, lambda: v.tensor_scalar(stv[:], stv[:], LN_EPS, None, ALU.add), dep=DV.n)
                    block.vector(_st)
                    var_dv = DV.n

                    def _l1(s, var_dv=var_dv):
                        s.wait_ge(DV.h, var_dv)
                        a_op(s, lambda: s.activation(stl[:], stv[:], AF.Ln))
                    block.scalar(_l1)
                    ln_ac = ACS.n

                    def _l2(v, ln_ac=ln_ac):
                        v.wait_ge(ACS.h, ln_ac)
                        d_op(v, lambda: v.tensor_scalar(stl[:], stl[:], -0.5, None, ALU.mult))
                    block.vector(_l2)
                    lh_dv = DV.n

                    def _l3(s, lh_dv=lh_dv):
                        s.wait_ge(DV.h, lh_dv)
                        a_op(s, lambda: s.activation(str_[:], stl[:], AF.Exp))
                    block.scalar(_l3)
                    rstd_ac = ACS.n

                    def _nm(v, gb=gb, rstd_ac=rstd_ac):
                        v.wait_ge(ACS.h, rstd_ac)
                        if gb >= 2:
                            v.wait_ge(PE.h, trlast_pe[gb % 2])
                        d_op(v, lambda: v.tensor_scalar(stn[:], stm[:], str_[:], -1.0, ALU.mult, ALU.mult))
                        d_op(v, lambda: v.tensor_scalar(XNT[gb % 2][:], XB[gb % 2][:], str_[:], stn[:],
                                                        ALU.mult, ALU.add), dep=DV.n)
                        norm_dv[gb % 2] = DV.n
                    block.vector(_nm)
                    nrm_here = DV.n

                    for q2 in range(16):
                        views = []

                        def _tp(t, q2=q2, gb=gb, b=b, nrm_here=nrm_here):
                            t.wait_ge(DV.h, nrm_here)
                            if b == 0 and q2 == 0:
                                t.wait_ge(S["SXN"].h, xnf_spill_sx)
                                t.wait_ge(PE.h, qkv_last_pe)
                            for k in range(2):
                                c = q2 * 2 + k
                                view, n = emit_transpose(t, XNT[gb % 2][:, c * 128:(c + 1) * 128], pTR)
                                views.append((view, n, c))
                        block.tensor(_tp)

                        def _ev(v, views=views, b=b):
                            for view, n, c in views:
                                v.wait_ge(PE.h, tr_state["tr_pe"][n])
                                dst = XNF[:, c * TILE + b * 128: c * TILE + b * 128 + 128]
                                d_op(v, lambda view=view, dst=dst: v.tensor_copy(dst, view))
                                tr_state["ev_dv"].append(DV.n)
                        block.vector(_ev)
                    trlast_pe[gb % 2] = PE.n

                def _xs(sq, ti=ti):
                    sq.wait_ge(DV.h, DV.n)
                    dst = bass.AP(_th(xn_sp), ti * TILE,
                                  [[TOK, 128], [128 * TOK, ECH], [1, TILE]])
                    dma(sq, S["SXN"], dst, XNF[:])
                block.sync(_xs)
                xnf_spill_sx = S["SXN"].n

                ln_evacs_dv = DV.n
                for ci in range(24):
                    gi = ti * 24 + ci

                    def _mm(t, ti=ti, ci=ci, gi=gi, ln_evacs_dv=ln_evacs_dv):
                        sem, cnt = wld[gi]
                        t.wait_ge(sem.h, cnt)
                        t.wait_ge(DV.h, ln_evacs_dv)
                        if gi >= 2:
                            t.wait_ge(DV.h, evac_dv[gi - 2])
                        if ci == 0 and ti > 0:
                            t.wait_ge(PE.h, attn_av_pe)
                        ps = pQKV[gi % 2]
                        inst = None
                        for ec in range(ECH):
                            inst = t.matmul(ps[:, 0:TILE], WB[gi % 3][:, ec * 128:(ec + 1) * 128],
                                            XNF[:, ec * TILE:(ec + 1) * TILE],
                                            start=(ec == 0), stop=(ec == ECH - 1))
                        wmm_pe.append(p_mark(inst))
                    block.tensor(_mm)
                    if gi + 3 < NTILE * 24:
                        nxt = gi + 3
                        emit_w_load(nxt // 24, nxt % 24)

                    def _evc(v, ti=ti, ci=ci, gi=gi):
                        v.wait_ge(PE.h, wmm_pe[gi])
                        m = ci % 8
                        ps = pQKV[gi % 2]
                        if ci < 8:
                            dst = bass.AP(_th(QG32), m, [[TILE * 32, 128], [32, TILE]])
                            bias = bqg[:, m:m + 1]
                        elif ci < 16:
                            dst = bass.AP(_th(K32), m, [[TILE * 32, 128], [32, TILE]])
                            bias = bks[:, m:m + 1]
                        else:
                            dst = bass.AP(_th(VA), m, [[TILE * 9, 128], [9, TILE]])
                            bias = bvs[:, m:m + 1]
                        if gi == 0:
                            v.wait_ge(GM.h, gm_ready)
                        d_op(v, lambda: v.tensor_scalar(dst, ps[:, 0:TILE], bias, None, ALU.add))
                        evac_dv.append(DV.n)
                    block.vector(_evc)
                qkv_last_pe = PE.n
                qkv_evacs_qk = evac_dv[ti * 24 + 15]
                qkv_evacs_v = evac_dv[ti * 24 + 23]

                exp_ac = 0
                norm_g_dv = 0
                sc_pe = 0
                avl_pe = [0, 0]
                scl_pe = [0, 0]
                for gidx in range(NGRP):
                    tb = gidx * GRP

                    def _tp(t, gidx=gidx, qkv_evacs_qk=qkv_evacs_qk):
                        t.wait_ge(DV.h, qkv_evacs_qk)
                        emit_transpose(t, QG32[:, gidx * 128:(gidx + 1) * 128], pTR)
                        emit_transpose(t, K32[:, gidx * 128:(gidx + 1) * 128], pTR)
                    block.tensor(_tp)
                    nq = tr_state["idx"] - 2

                    def _ev(v, nq=nq, gidx=gidx):
                        for off, dstt in ((0, QGT[gidx % 2]), (1, KTT[gidx % 2])):
                            n = nq + off
                            v.wait_ge(PE.h, tr_state["tr_pe"][n])
                            if gidx >= 2:
                                v.wait_ge(PE.h, scl_pe[gidx % 2])
                            bank = pTR[n % 2]
                            slot = (n // 2) % 4
                            view = bank[:, slot * 128:(slot + 1) * 128]
                            d_op(v, lambda view=view, dstt=dstt: v.tensor_copy(dstt[:], view))
                            tr_state["ev_dv"].append(DV.n)
                    block.vector(_ev)
                    evk_dv = DV.n

                    def _sc(t, gidx=gidx, evk_dv=evk_dv, exp_ac=exp_ac, norm_g_dv=norm_g_dv):
                        t.wait_ge(DV.h, evk_dv)
                        if gidx >= 1:
                            t.wait_ge(ACS.h, exp_ac)
                            t.wait_ge(DV.h, norm_g_dv)
                        inst = None
                        for j in range(GRP):
                            inst = t.matmul(pSC[:, j * 512:j * 512 + 128],
                                            KTT[gidx % 2][32 * j:32 * j + 8, :],
                                            QGT[gidx % 2][32 * j:32 * j + 8, :],
                                            start=True, stop=True, tile_position=(32 * j, 0))
                        p_mark(inst)
                        scl_pe[gidx % 2] = PE.n
                    block.tensor(_sc)
                    sc_pe = PE.n

                    def _ex(s, sc_pe=sc_pe, gidx=gidx):
                        s.wait_ge(PE.h, sc_pe)
                        if gidx >= 2:
                            s.wait_ge(PE.h, avl_pe[gidx % 2])
                        src = bass.AP(pSCh, 0, [[2048, 128], [512, 4], [1, 128]])
                        a_op(s, lambda: s.activation(ET[gidx % 2][:], src, AF.Exp))
                    block.scalar(_ex)
                    exp_ac = ACS.n

                    def _av(t, gidx=gidx, tb=tb, exp_ac=exp_ac, qkv_evacs_v=qkv_evacs_v,
                            norm_g_dv=norm_g_dv):
                        t.wait_ge(ACS.h, exp_ac)
                        t.wait_ge(DV.h, qkv_evacs_v)
                        if gidx >= 1:
                            t.wait_ge(DV.h, norm_g_dv)
                        inst = None
                        for j in range(GRP):
                            inst = t.matmul(pSC[:, 128 + j * 9:128 + (j + 1) * 9],
                                            ET[gidx % 2][:, j * 128:(j + 1) * 128],
                                            VA[:, (tb + j) * 9:(tb + j + 1) * 9],
                                            start=True, stop=True)
                        p_mark(inst)
                        avl_pe[gidx % 2] = PE.n
                    block.tensor(_av)
                    av_pe = PE.n

                    def _nr(v, av_pe=av_pe, tb=tb, ti=ti, gidx=gidx):
                        v.wait_ge(PE.h, av_pe)
                        if gidx == 0:
                            v.wait_ge(S["GSP"].h, o8_spill_gs[ti % 2])
                        rb = REC[gidx % 2]
                        d_op(v, lambda: v.reciprocal(rb[:], bass.AP(pSCh, 136, [[2048, 128], [9, 4]])))
                        outv = bass.AP(_th(O8N[ti % 2]), tb, [[8 * TILE, 128], [TILE, 8], [1, 4]])
                        inv = bass.AP(pSCh, 128, [[2048, 128], [1, 8], [9, 4]])
                        recv = bass.AP(_th(rb), 0, [[4, 128], [0, 8], [1, 4]])
                        d_op(v, lambda: v.tensor_tensor(outv, inv, recv, ALU.mult), dep=DV.n)
                    block.vector(_nr)
                    norm_g_dv = DV.n
                attn_av_pe = PE.n
                attn_norm_dv = DV.n

                def _as(gq, ti=ti, attn_norm_dv=attn_norm_dv):
                    gq.wait_ge(DV.h, attn_norm_dv)
                    for j in range(4):
                        dst = bass.AP(_th(attn_sp), j * TOK + ti * TILE,
                                      [[32 * TOK, 128], [4 * TOK, 8], [1, TILE]])
                        dma(gq, S["GSP"], dst, O8N[ti % 2][:])
                block.gpsimd(_as)
                o8_spill_gs[ti % 2] = S["GSP"].n

        barrier()

        with contextlib.ExitStack() as scB:
            eB = scB.enter_context
            XN3 = eB(nc.sbuf_tensor("XN3", [128, ECH * TOK], BF16))
            WUPB = [eB(nc.sbuf_tensor(f"WUPB{i}", [128, 4096], BF16)) for i in range(2)]
            WGTB = [eB(nc.sbuf_tensor(f"WGTB{i}", [128, 4096], BF16)) for i in range(2)]
            SG = [eB(nc.sbuf_tensor(f"SG{i}", [128, 512], F32)) for i in range(2)]
            HCH = [eB(nc.sbuf_tensor(f"HCH{i}", [128, TOK], BF16)) for i in range(2)]
            WDS = [eB(nc.sbuf_tensor(f"WDS{i}", [128, 4096], BF16)) for i in range(2)]
            pUP = [eB(nc.psum_tensor(f"pUP{i}", [128, 512], F32)) for i in range(2)]
            pGA = [eB(nc.psum_tensor(f"pGA{i}", [128, 512], F32)) for i in range(2)]

            def _x3(sq):
                src = bass.AP(_th(xn_sp), 0, [[TOK, 128], [128 * TOK, ECH], [1, TOK]])
                dma(sq, S["SH3"], XN3[:], src)
            block.sync(_x3)
            xn3_ld = S["SH3"].n

            upg_ld = []
            mm_up_pe = {}
            mm_ga_pe = {}
            silu_ac = {}
            mult_dv = {}
            hsp_sx = [0, 0]
            wd_ld_gp = [0] * FCH
            wd_st_sw = [0] * FCH

            def emit_upg_load(fc):
                def _w(gq, fc=fc):
                    if fc >= 2:
                        gq.wait_ge(PE.h, mm_ga_pe[(fc - 2, TG3 - 1)])
                    for wsrc, wdst in ((wupT, WUPB[fc % 2]), (wgT, WGTB[fc % 2])):
                        src = bass.AP(_th(wsrc), fc * 128,
                                      [[DFF, 128], [128 * DFF, ECH], [1, 128]])
                        upg_ld.append(dma(gq, S["GUP"], wdst[:], src))
                block.gpsimd(_w)

            emit_upg_load(0)
            emit_upg_load(1)

            for fc in range(FCH):
                ld_cnt = upg_ld[fc * 2 + 1]
                for tg in range(TG3):
                    def _mm(t, fc=fc, tg=tg, ld_cnt=ld_cnt):
                        t.wait_ge(S["GUP"].h, ld_cnt)
                        if fc == 0 and tg == 0:
                            t.wait_ge(S["SH3"].h, xn3_ld)
                        key2 = (fc, tg - 2) if tg >= 2 else (fc - 1, tg + 2)
                        if key2 in mult_dv:
                            t.wait_ge(DV.h, mult_dv[key2])
                        if key2 in silu_ac:
                            t.wait_ge(ACS.h, silu_ac[key2])
                        inst = None
                        for ec in range(ECH):
                            rhs = XN3[:, ec * TOK + tg * 512: ec * TOK + tg * 512 + 512]
                            inst = t.matmul(pUP[tg % 2][:], WUPB[fc % 2][:, ec * 128:(ec + 1) * 128],
                                            rhs, start=(ec == 0), stop=(ec == ECH - 1))
                        mm_up_pe[(fc, tg)] = p_mark(inst)
                        inst = None
                        for ec in range(ECH):
                            rhs = XN3[:, ec * TOK + tg * 512: ec * TOK + tg * 512 + 512]
                            inst = t.matmul(pGA[tg % 2][:], WGTB[fc % 2][:, ec * 128:(ec + 1) * 128],
                                            rhs, start=(ec == 0), stop=(ec == ECH - 1))
                        mm_ga_pe[(fc, tg)] = p_mark(inst)
                    block.tensor(_mm)

                    def _si(s, fc=fc, tg=tg):
                        s.wait_ge(PE.h, mm_ga_pe[(fc, tg)])
                        key2 = (fc, tg - 2) if tg >= 2 else (fc - 1, tg + 2)
                        if key2 in mult_dv:
                            s.wait_ge(DV.h, mult_dv[key2])
                        silu_ac[(fc, tg)] = a_op(
                            s, lambda: s.activation(SG[tg % 2][:], pGA[tg % 2][:], AF.Silu))
                    block.scalar(_si)

                    def _ml(v, fc=fc, tg=tg):
                        v.wait_ge(PE.h, mm_up_pe[(fc, tg)])
                        v.wait_ge(ACS.h, silu_ac[(fc, tg)])
                        if tg == 0:
                            v.wait_ge(S["SH"].h, hsp_sx[fc % 2])
                        dst = HCH[fc % 2][:, tg * 512:(tg + 1) * 512]
                        mult_dv[(fc, tg)] = d_op(
                            v, lambda: v.tensor_tensor(dst, pUP[tg % 2][:], SG[tg % 2][:], ALU.mult))
                    block.vector(_ml)

                def _hs(sq, fc=fc):
                    sq.wait_ge(DV.h, mult_dv[(fc, TG3 - 1)])
                    dst = bass.AP(_th(h_sp), fc * 128 * TOK, [[TOK, 128], [1, TOK]])
                    dma(sq, S["SH"], dst, HCH[fc % 2][:])
                block.sync(_hs)
                hsp_sx[fc % 2] = S["SH"].n

                if fc + 2 < FCH:
                    emit_upg_load(fc + 2)

                def _wdl(gq, fc=fc):
                    if fc >= 2:
                        gq.wait_ge(S["SWD"].h, wd_st_sw[fc - 2])
                    src = bass.AP(_th(wdT), fc * 128 * EMB, [[EMB, 128], [1, EMB]])
                    wd_ld_gp[fc] = dma(gq, S["GWD"], WDS[fc % 2][:], src)
                block.gpsimd(_wdl)

                def _wds(sq, fc=fc):
                    sq.wait_ge(S["GWD"].h, wd_ld_gp[fc])
                    dst = bass.AP(_th(wd_bf), fc * 128 * EMB, [[EMB, 128], [1, EMB]])
                    wd_st_sw[fc] = dma(sq, S["SWD"], dst, WDS[fc % 2][:])
                block.sync(_wds)

        barrier()

        with contextlib.ExitStack() as scC:
            eC = scC.enter_context
            H3B = eC(nc.sbuf_tensor("H3B", [128, FCH * 512], BF16))
            WDB = [eC(nc.sbuf_tensor(f"WDB{i}", [128, FCH * 128], BF16)) for i in range(2)]
            YB = [eC(nc.sbuf_tensor(f"YB{i}", [128, 512], F32)) for i in range(2)]
            pY = [eC(nc.psum_tensor(f"pY{i}", [128, 512], F32)) for i in range(2)]

            ymm_pe = {}
            yev_dv = {}
            ysp_sx = {}
            h3_ld = [0] * SUB
            wd_ld = {}

            def emit_wd_load(sub, dm):
                def _w(sq, sub=sub, dm=dm):
                    gi = sub * DMCH + dm
                    if gi >= 2:
                        pgi = gi - 2
                        sq.wait_ge(PE.h, ymm_pe[(pgi // DMCH, pgi % DMCH)])
                    src = bass.AP(_th(wd_bf), dm * 128,
                                  [[EMB, 128], [128 * EMB, FCH], [1, 128]])
                    wd_ld[(sub, dm)] = dma(sq, S["SWB"], WDB[gi % 2][:], src)
                block.sync(_w)

            FH = 43
            for sub in range(SUB):
                def _h3(sq, sub=sub):
                    if sub >= 1:
                        sq.wait_ge(PE.h, ymm_pe[(sub - 1, DMCH - 1)])
                    srcA = bass.AP(_th(h_sp), sub * 512,
                                   [[TOK, 128], [128 * TOK, FH], [1, 512]])
                    ca = dma(sq, S["SH3"], H3B[:, 0:FH * 512], srcA)
                    srcB = bass.AP(_th(h_sp), FH * 128 * TOK + sub * 512,
                                   [[TOK, 128], [128 * TOK, FCH - FH], [1, 512]])
                    cb = dma(sq, S["SH3"], H3B[:, FH * 512:], srcB)
                    h3_ld[sub] = (ca, cb)
                block.sync(_h3)
                if sub == 0:
                    emit_wd_load(0, 0)
                    emit_wd_load(0, 1)

                for dm in range(DMCH):
                    gi = sub * DMCH + dm

                    def _mm(t, sub=sub, dm=dm, gi=gi):
                        t.wait_ge(S["SWB"].h, wd_ld[(sub, dm)])
                        t.wait_ge(S["SH3"].h, h3_ld[sub][0])
                        key2 = (sub, dm - 2) if dm >= 2 else (sub - 1, dm + DMCH - 2)
                        if key2 in yev_dv:
                            t.wait_ge(DV.h, yev_dv[key2])
                        inst = None
                        for fcc in range(FH):
                            inst = t.matmul(pY[gi % 2][:], WDB[gi % 2][:, fcc * 128:(fcc + 1) * 128],
                                            H3B[:, fcc * 512:(fcc + 1) * 512],
                                            start=(fcc == 0), stop=False)
                        t.wait_ge(S["SH3"].h, h3_ld[sub][1])
                        for fcc in range(FH, FCH):
                            inst = t.matmul(pY[gi % 2][:], WDB[gi % 2][:, fcc * 128:(fcc + 1) * 128],
                                            H3B[:, fcc * 512:(fcc + 1) * 512],
                                            start=False, stop=(fcc == FCH - 1))
                        ymm_pe[(sub, dm)] = p_mark(inst)
                    block.tensor(_mm)
                    ngi = gi + 2
                    if ngi < SUB * DMCH:
                        emit_wd_load(ngi // DMCH, ngi % DMCH)

                    def _ev(v, sub=sub, dm=dm, gi=gi):
                        v.wait_ge(PE.h, ymm_pe[(sub, dm)])
                        key2 = (sub, dm - 2) if dm >= 2 else (sub - 1, dm + DMCH - 2)
                        if key2 in ysp_sx:
                            v.wait_ge(S["SY"].h, ysp_sx[key2])
                        yev_dv[(sub, dm)] = d_op(
                            v, lambda: v.tensor_copy(YB[gi % 2][:], pY[gi % 2][:]))
                    block.vector(_ev)

                    def _ys(gq, sub=sub, dm=dm, gi=gi):
                        gq.wait_ge(DV.h, yev_dv[(sub, dm)])
                        dst = bass.AP(_th(y_sp), dm * 128 * TOK + sub * 512,
                                      [[TOK, 128], [1, 512]])
                        ysp_sx[(sub, dm)] = dma(gq, S["SY"], dst, YB[gi % 2][:])
                    block.gpsimd(_ys)

        barrier()

        with contextlib.ExitStack() as scD:
            eD = scD.enter_context
            YL = [eD(nc.sbuf_tensor(f"YL{i}", [128, EMB], F32)) for i in range(2)]
            AL = [eD(nc.sbuf_tensor(f"AL{i}", [128, EMB], F32)) for i in range(2)]
            XL = [eD(nc.sbuf_tensor(f"XL{i}", [128, EMB], F32)) for i in range(2)]
            SFM = eD(nc.sbuf_tensor("SFM", [128, EMB], F32))
            OT = [eD(nc.sbuf_tensor(f"OT{i}", [128, EMB], F32)) for i in range(2)]
            pTC = [eD(nc.psum_tensor(f"pTC{i}", [128, 512], F32)) for i in range(2)]

            add_dv = [0] * (TOK // 128)
            trc_pe = {}
            oadd_dv = {}
            out_g = [0, 0]
            sfm_tr_pe = 0

            for blk in range(TOK // 128):
                def _ld(sq, blk=blk):
                    if blk >= 2:
                        sq.wait_ge(DV.h, add_dv[blk - 2])
                        sq.wait_ge(S["GOUT"].h, out_g[blk % 2])
                    ysrc = bass.AP(_th(y_sp), blk * 128,
                                   [[TOK, 128], [128 * TOK, ECH], [1, 128]])
                    dma(sq, S["SCL"], YL[blk % 2][:], ysrc)
                    asrc = bass.AP(_th(attn_sp), blk * 128,
                                   [[TOK, 128], [128 * TOK, ECH], [1, 128]])
                    dma(sq, S["SCL"], AL[blk % 2][:], asrc)
                    xsrc = bass.AP(_th(x_in), blk * 128 * EMB, [[EMB, 128], [1, EMB]])
                    dma(sq, S["SCL"], XL[blk % 2][:], xsrc)
                block.sync(_ld)
                here = S["SCL"].n

                def _ad(v, blk=blk, here=here, sfm_tr_pe=sfm_tr_pe):
                    v.wait_ge(S["SCL"].h, here)
                    if blk >= 1:
                        v.wait_ge(PE.h, sfm_tr_pe)
                    add_dv[blk] = d_op(
                        v, lambda: v.tensor_tensor(SFM[:], YL[blk % 2][:], AL[blk % 2][:], ALU.add))
                block.vector(_ad)

                for q2 in range(16):
                    def _tp(t, blk=blk, q2=q2):
                        t.wait_ge(DV.h, add_dv[blk])
                        for k in range(2):
                            c = q2 * 2 + k
                            n = blk * 32 + c
                            bank = pTC[n % 2]
                            slot = (n // 2) % 4
                            if n >= 2:
                                t.wait_ge(DV.h, oadd_dv.get(n - 2, 0))
                            inst = t.transpose(bank[:, slot * 128:(slot + 1) * 128],
                                               SFM[:, c * 128:(c + 1) * 128], idf[:])
                            trc_pe[n] = p_mark(inst)
                    block.tensor(_tp)

                    def _oa(v, blk=blk, q2=q2):
                        for k in range(2):
                            c = q2 * 2 + k
                            n = blk * 32 + c
                            if c == 0:
                                v.wait_ge(S["GOUT"].h, out_g[blk % 2])
                            v.wait_ge(PE.h, trc_pe[n])
                            bank = pTC[n % 2]
                            slot = (n // 2) % 4
                            view = bank[:, slot * 128:(slot + 1) * 128]
                            dst = OT[blk % 2][:, c * 128:(c + 1) * 128]
                            xsl = XL[blk % 2][:, c * 128:(c + 1) * 128]
                            oadd_dv[n] = d_op(
                                v, lambda view=view, dst=dst, xsl=xsl:
                                v.tensor_tensor(dst, view, xsl, ALU.add))
                    block.vector(_oa)
                sfm_tr_pe = PE.n

                def _ow(gq, blk=blk):
                    gq.wait_ge(DV.h, oadd_dv[blk * 32 + 31])
                    dst = bass.AP(_th(out_h), blk * 128 * EMB, [[EMB, 128], [1, EMB]])
                    dma(gq, S["GOUT"], dst, OT[blk % 2][:])
                block.gpsimd(_ow)
                out_g[blk % 2] = S["GOUT"].n

            def _fin(gq):
                gq.wait_ge(S["GOUT"].h, S["GOUT"].n)
            block.gpsimd(_fin)

    return nc


def kernel(L0_attn_x, L0_attn_Wq, L0_attn_bq, L0_attn_Wk, L0_attn_bk,
           L0_attn_Wv, L0_attn_bv, L0_ffn_W_gate, L0_ffn_W_up, L0_ffn_W_down):
    x = np.asarray(L0_attn_x, dtype=np.float32).reshape(16384, 4096)
    wq = np.asarray(L0_attn_Wq, dtype=np.float32)
    wk = np.asarray(L0_attn_Wk, dtype=np.float32)
    wv = np.asarray(L0_attn_Wv, dtype=np.float32)
    shared = {
        "wq_r": np.ascontiguousarray(np.transpose(wq, (0, 2, 1)).reshape(4096, 4096)),
        "wk_r": np.ascontiguousarray(np.transpose(wk, (0, 2, 1)).reshape(4096, 1024)),
        "wv_r": np.ascontiguousarray(np.transpose(wv, (0, 2, 1)).reshape(4096, 1024)),
        "bq": np.ascontiguousarray(np.asarray(L0_attn_bq, dtype=np.float32)),
        "bk": np.ascontiguousarray(np.asarray(L0_attn_bk, dtype=np.float32)),
        "bv": np.ascontiguousarray(np.asarray(L0_attn_bv, dtype=np.float32)),
        "wupT": np.ascontiguousarray(np.asarray(L0_ffn_W_up, dtype=np.float32).T),
        "wgT": np.ascontiguousarray(np.asarray(L0_ffn_W_gate, dtype=np.float32).T),
        "wdT": np.ascontiguousarray(np.asarray(L0_ffn_W_down, dtype=np.float32).T),
        "idn": np.eye(128, dtype=np.float32),
    }
    nc = build_nc()
    in_maps = []
    for c in range(8):
        m = dict(shared)
        m["x"] = np.ascontiguousarray(x[c * TOK:(c + 1) * TOK])
        in_maps.append(m)
    res = run_bass_kernel_spmd(nc, in_maps, core_ids=list(range(8)))
    out = np.concatenate([r["out"] for r in res.results], axis=0)
    return out.reshape(1, 16384, 4096)


# version 06
# speedup vs baseline: 1.0252x; 1.0252x over previous
import numpy as np
import contextlib
import concourse.bass as bass
import concourse.mybir as mybir
from concourse.bass_utils import run_bass_kernel_spmd

F32 = mybir.dt.float32
BF16 = mybir.dt.bfloat16
AF = mybir.ActivationFunctionType
ALU = mybir.AluOpType
AX = mybir.AxisListType

TOK = 2048
EMB = 4096
ECH = 32
DFF = 11008
FCH = 86
TILE = 256
NTILE = TOK // TILE
BLK = 128
NBLK = TOK // BLK
GRP = 4
NGRP = TILE // GRP
TG3 = 4
SUB = 4
DMCH = 32
INV_SQRT_D = 0.08838834764831845
INV_EMB = 1.0 / 4096.0
LN_EPS = 1e-5


def _th(x):
    return x.tensor if hasattr(x, "tensor") else x


class _Ctr:
    __slots__ = ("h", "n")

    def __init__(self, h):
        self.h = h
        self.n = 0


def build_nc():
    nc = bass.Bass("TRN2", target_bir_lowering=False)

    x_in = nc.dram_tensor("x", [TOK, EMB], F32, kind="ExternalInput")
    wq_r = nc.dram_tensor("wq_r", [EMB, 4096], F32, kind="ExternalInput")
    wk_r = nc.dram_tensor("wk_r", [EMB, 1024], F32, kind="ExternalInput")
    wv_r = nc.dram_tensor("wv_r", [EMB, 1024], F32, kind="ExternalInput")
    bq_in = nc.dram_tensor("bq", [128, 32], F32, kind="ExternalInput")
    bk_in = nc.dram_tensor("bk", [128, 8], F32, kind="ExternalInput")
    bv_in = nc.dram_tensor("bv", [128, 8], F32, kind="ExternalInput")
    wupT = nc.dram_tensor("wupT", [EMB, DFF], F32, kind="ExternalInput")
    wgT = nc.dram_tensor("wgT", [EMB, DFF], F32, kind="ExternalInput")
    wdT = nc.dram_tensor("wdT", [DFF, EMB], F32, kind="ExternalInput")
    idn = nc.dram_tensor("idn", [128, 128], F32, kind="ExternalInput")
    out_h = nc.dram_tensor("out", [TOK, EMB], F32, kind="ExternalOutput")

    wqg_bf = nc.dram_tensor("wqg_bf", [EMB, 1024], BF16, kind="Internal")
    wd_bf = nc.dram_tensor("wd_bf", [DFF, EMB], BF16, kind="Internal")
    xn_sp = nc.dram_tensor("xn_sp", [EMB, TOK], BF16, kind="Internal")
    attn_sp = nc.dram_tensor("attn_sp", [EMB, TOK], F32, kind="Internal")
    h_sp = nc.dram_tensor("h_sp", [DFF, TOK], BF16, kind="Internal")
    y_sp = nc.dram_tensor("y_sp", [EMB, TOK], F32, kind="Internal")

    g = contextlib.ExitStack()
    with g:
        es = g.enter_context
        idb = es(nc.sbuf_tensor("idb", [128, 128], BF16))
        idf = es(nc.sbuf_tensor("idf", [128, 128], F32))
        bqs = es(nc.sbuf_tensor("bqs", [128, 32], F32))
        bks = es(nc.sbuf_tensor("bks", [128, 8], F32))
        bvs = es(nc.sbuf_tensor("bvs", [128, 8], F32))
        bqg = es(nc.sbuf_tensor("bqg", [128, 8], F32))
        bt1 = es(nc.sbuf_tensor("bt1", [128, 8], F32))
        bt2 = es(nc.sbuf_tensor("bt2", [128, 8], F32))

        names = (["DV", "AC", "PE", "GM"]
                 + ["SX0", "SX1", "SW0", "SW1", "SW2", "SXN0", "SXN1",
                    "GKV0", "GKV1", "GKV2", "GSP0", "GSP1"]
                 + ["SQS0", "SQS1", "SQW0", "SQW1", "SIN", "GIN"]
                 + ["GUP0", "GUP1", "SH0", "SH1", "GWD0", "GWD1", "SWD0", "SWD1"]
                 + ["SH3", "SH3B", "SWB0", "SWB1", "SY0", "SY1"]
                 + ["SCL0", "SCL1", "GOUT0", "GOUT1"])
        S = {n: _Ctr(es(nc.semaphore(n))) for n in names}
        DV, ACS, PE, GM = S["DV"], S["AC"], S["PE"], S["GM"]

        block = es(nc.Block())

        def d_op(v, fn, dep=None):
            if dep is not None:
                v.wait_ge(DV.h, dep)
            fn().then_inc(DV.h, 1)
            DV.n += 1
            return DV.n

        def a_op(s, fn):
            s.wait_ge(ACS.h, ACS.n)
            fn().then_inc(ACS.h, 1)
            ACS.n += 1
            return ACS.n

        def p_mark(inst):
            inst.then_inc(PE.h, 1)
            PE.n += 1
            return PE.n

        def dma(q, sem, out, in_):
            q.dma_start(out=out, in_=in_).then_inc(sem.h, 16)
            sem.n += 16
            return sem.n

        def barrier():
            cnts = [(c.h, c.n) for c in S.values()]
            for engname in ("tensor", "vector", "scalar", "sync", "gpsimd"):
                def _mk(e):
                    for h, n in cnts:
                        e.wait_ge(h, n)
                getattr(block, engname)(_mk)

        tr_state = {"idx": 0, "ev_dv": [], "tr_pe": []}

        def emit_transpose(t, src, psums):
            n = tr_state["idx"]
            bank = psums[n % 2]
            slot = (n // 2) % 4
            view = bank[:, slot * 128:(slot + 1) * 128]
            if n >= 2:
                t.wait_ge(DV.h, tr_state["ev_dv"][n - 2])
            inst = t.transpose(view, src, idb[:])
            p_mark(inst)
            tr_state["tr_pe"].append(PE.n)
            tr_state["idx"] += 1
            return view, n

        with contextlib.ExitStack() as sc0:
            e0 = sc0.enter_context
            WQS = [e0(nc.sbuf_tensor(f"WQS{i}", [128, 4096], F32)) for i in range(2)]
            s01 = e0(nc.sbuf_tensor("s01", [128, 1024], F32))
            s23 = e0(nc.sbuf_tensor("s23", [128, 1024], F32))
            ssum = e0(nc.sbuf_tensor("ssum", [128, 1024], F32))
            wqg_t = [e0(nc.sbuf_tensor(f"wqgt{i}", [128, 1024], BF16)) for i in range(2)]

            def _gp(gq):
                dma(gq, S["GIN"], idb[:], idn.ap())
            block.gpsimd(_gp)

            def _sy(sq):
                dma(sq, S["SIN"], idf[:], idn.ap())
                dma(sq, S["SIN"], bqs[:], bq_in.ap())
                dma(sq, S["SIN"], bks[:], bk_in.ap())
                dma(sq, S["SIN"], bvs[:], bv_in.ap())
            block.sync(_sy)

            def _dv(v):
                v.wait_ge(S["SIN"].h, S["SIN"].n)
                v.wait_ge(S["GIN"].h, S["GIN"].n)
                bqh = _th(bqs)
                views = [bass.AP(bqh, j, [[32, 128], [4, 8]]) for j in range(4)]
                d_op(v, lambda: v.tensor_tensor(bt1[:], views[0], views[1], ALU.add))
                d_op(v, lambda: v.tensor_tensor(bt2[:], views[2], views[3], ALU.add))
                d_op(v, lambda: v.tensor_tensor(bt1[:], bt1[:], bt2[:], ALU.add), dep=DV.n)
                d_op(v, lambda: v.tensor_scalar(bqg[:], bt1[:], INV_SQRT_D, None, ALU.mult), dep=DV.n)
            block.vector(_dv)

            s23_dv = [0] * ECH
            st_sq = [0] * ECH
            ld_sq = [0] * ECH
            for ec in range(ECH):
                buf = ec % 2

                def _ld(sq, ec=ec, buf=buf):
                    if ec >= 2:
                        sq.wait_ge(DV.h, s23_dv[ec - 2])
                    src = bass.AP(_th(wq_r), ec * 128 * 4096, [[4096, 128], [1, 4096]])
                    ld_sq[ec] = dma(sq, S[f"SQS{buf}"], WQS[buf][:], src)
                block.sync(_ld)

                def _gs(v, ec=ec, buf=buf):
                    v.wait_ge(S[f"SQS{buf}"].h, ld_sq[ec])
                    if ec >= 2:
                        v.wait_ge(S[f"SQW{buf}"].h, st_sq[ec - 2])
                    wh = _th(WQS[buf])
                    vw = [bass.AP(wh, j * 128, [[4096, 128], [512, 8], [1, 128]]) for j in range(4)]
                    d_op(v, lambda: v.tensor_tensor(s01[:], vw[0], vw[1], ALU.add))
                    d_op(v, lambda: v.tensor_tensor(s23[:], vw[2], vw[3], ALU.add))
                    s23_dv[ec] = DV.n
                    d_op(v, lambda: v.tensor_tensor(ssum[:], s01[:], s23[:], ALU.add), dep=DV.n)
                    d_op(v, lambda: v.tensor_scalar(wqg_t[buf][:], ssum[:], INV_SQRT_D, None, ALU.mult), dep=DV.n)
                block.vector(_gs)

                def _st(sq, ec=ec, buf=buf):
                    sq.wait_ge(DV.h, DV.n)
                    dst = bass.AP(_th(wqg_bf), ec * 128 * 1024, [[1024, 128], [1, 1024]])
                    st_sq[ec] = dma(sq, S[f"SQW{buf}"], dst, wqg_t[buf][:])
                block.sync(_st)

        barrier()

        with contextlib.ExitStack() as scA:
            eA = scA.enter_context
            XB = [eA(nc.sbuf_tensor(f"XB{i}", [128, EMB], F32)) for i in range(2)]
            XSQ = eA(nc.sbuf_tensor("XSQ", [128, EMB], F32))
            XNT = [eA(nc.sbuf_tensor(f"XNT{i}", [128, EMB], BF16)) for i in range(2)]
            XNF = eA(nc.sbuf_tensor("XNF", [128, ECH * TILE], BF16))
            WB = [eA(nc.sbuf_tensor(f"WB{i}", [128, 4096], BF16)) for i in range(3)]
            QG32 = [eA(nc.sbuf_tensor(f"QG32{i}", [128, TILE * 32], BF16)) for i in range(2)]
            K32 = [eA(nc.sbuf_tensor(f"K32{i}", [128, TILE * 32], BF16)) for i in range(2)]
            VA = [eA(nc.sbuf_tensor(f"VA{i}", [128, TILE * 9], BF16)) for i in range(2)]
            QGT = [eA(nc.sbuf_tensor(f"QGT{i}", [128, 128], BF16)) for i in range(2)]
            KTT = [eA(nc.sbuf_tensor(f"KTT{i}", [128, 128], BF16)) for i in range(2)]
            ET = [eA(nc.sbuf_tensor(f"ET{i}", [128, 512], BF16)) for i in range(2)]
            O8N = [eA(nc.sbuf_tensor(f"O8N{i}", [128, 8 * TILE], F32)) for i in range(2)]
            REC = [eA(nc.sbuf_tensor(f"REC{i}", [128, 4], F32)) for i in range(2)]
            st1 = eA(nc.sbuf_tensor("st1", [128, 1], F32))
            st2 = eA(nc.sbuf_tensor("st2", [128, 1], F32))
            stm = eA(nc.sbuf_tensor("stm", [128, 1], F32))
            stv = eA(nc.sbuf_tensor("stv", [128, 1], F32))
            str_ = eA(nc.sbuf_tensor("str", [128, 1], F32))
            stl = eA(nc.sbuf_tensor("stl", [128, 1], F32))
            stn = eA(nc.sbuf_tensor("stn", [128, 1], F32))
            pQKV = [eA(nc.psum_tensor(f"pQKV{i}", [128, 512], F32)) for i in range(2)]
            pTR = [eA(nc.psum_tensor(f"pTR{i}", [128, 512], BF16)) for i in range(2)]
            pSC = eA(nc.psum_tensor("pSC", [128, 2048], F32))
            pSCh = _th(pSC)

            def _ini(gq):
                for i in range(2):
                    gq.memset(QG32[i][:], 0.0).then_inc(GM.h, 1); GM.n += 1
                    gq.memset(K32[i][:], 0.0).then_inc(GM.h, 1); GM.n += 1
                    gq.memset(bass.AP(_th(VA[i]), 8, [[TILE * 9, 128], [9, TILE]]),
                              1.0).then_inc(GM.h, 1)
                    GM.n += 1
            block.gpsimd(_ini)
            gm_ready = GM.n

            norm_dv = [0, 0]
            x_ld = [0] * (2 * NTILE + 2)
            trlast_pe = [0, 0]
            xn_spill_sx = [0] * (NTILE + 1)
            qkv_mm_last = [0] * (NTILE + 1)
            qkv_ev_qk = [0] * (NTILE + 1)
            qkv_ev_v = [0] * (NTILE + 1)
            attn_done_pe = [0, 0]
            o8_spill_gs = [0, 0]
            wld = []
            wmm_pe = []
            evac_dv = []
            A = {"exp_ac": 0, "norm_dv": 0, "scl_pe": [0, 0], "avl_pe": [0, 0]}

            def emit_w_load(ti, ci):
                gi = ti * 24 + ci
                m = ci % 8
                if ci < 8:
                    eng = "sync"
                    sem = S[f"SW{gi % 3}"]
                    src = bass.AP(_th(wqg_bf), m * 128,
                                  [[1024, 128], [128 * 1024, 32], [1, 128]])
                else:
                    eng = "gpsimd"
                    sem = S[f"GKV{gi % 3}"]
                    src = bass.AP(_th(wk_r if ci < 16 else wv_r), m * 128,
                                  [[1024, 128], [128 * 1024, 32], [1, 128]])

                def _w(q, gi=gi, sem=sem, src=src):
                    if gi >= 3:
                        q.wait_ge(PE.h, wmm_pe[gi - 3])
                    wld.append((sem, dma(q, sem, WB[gi % 3][:], src)))
                getattr(block, eng)(_w)

            def emit_tile_head(ti):
                def _xl(sq, ti=ti):
                    for b in range(2):
                        gb = ti * 2 + b
                        if gb >= 2:
                            sq.wait_ge(DV.h, norm_dv[gb % 2])
                        src = bass.AP(_th(x_in), gb * 128 * EMB, [[EMB, 128], [1, EMB]])
                        x_ld[gb] = dma(sq, S[f"SX{gb % 2}"], XB[gb % 2][:], src)
                block.sync(_xl)
                if ti == 0:
                    for ci in range(3):
                        emit_w_load(0, ci)

            def emit_ln_stats(ti, b):
                gb = ti * 2 + b

                def _st(v, gb=gb):
                    v.wait_ge(S[f"SX{gb % 2}"].h, x_ld[gb])
                    xb = XB[gb % 2]
                    d_op(v, lambda: v.reduce_sum(st1[:], xb[:], AX.X))
                    d_op(v, lambda: v.tensor_tensor(XSQ[:], xb[:], xb[:], ALU.mult))
                    d_op(v, lambda: v.reduce_sum(st2[:], XSQ[:], AX.X), dep=DV.n)
                    d_op(v, lambda: v.tensor_scalar(stm[:], st1[:], INV_EMB, None, ALU.mult), dep=DV.n)
                    d_op(v, lambda: v.tensor_scalar(st2[:], st2[:], INV_EMB, None, ALU.mult), dep=DV.n)
                    d_op(v, lambda: v.tensor_tensor(st1[:], stm[:], stm[:], ALU.mult), dep=DV.n)
                    d_op(v, lambda: v.tensor_tensor(stv[:], st2[:], st1[:], ALU.subtract), dep=DV.n)
                    d_op(v, lambda: v.tensor_scalar(stv[:], stv[:], LN_EPS, None, ALU.add), dep=DV.n)
                block.vector(_st)
                var_dv = DV.n

                def _l1(s, var_dv=var_dv):
                    s.wait_ge(DV.h, var_dv)
                    a_op(s, lambda: s.activation(stl[:], stv[:], AF.Ln))
                block.scalar(_l1)
                ln_ac = ACS.n

                def _l2(v, ln_ac=ln_ac):
                    v.wait_ge(ACS.h, ln_ac)
                    d_op(v, lambda: v.tensor_scalar(stl[:], stl[:], -0.5, None, ALU.mult))
                block.vector(_l2)
                lh_dv = DV.n

                def _l3(s, lh_dv=lh_dv):
                    s.wait_ge(DV.h, lh_dv)
                    a_op(s, lambda: s.activation(str_[:], stl[:], AF.Exp))
                block.scalar(_l3)
                rstd_ac = ACS.n

                def _nm(v, gb=gb, rstd_ac=rstd_ac):
                    v.wait_ge(ACS.h, rstd_ac)
                    if gb >= 2:
                        v.wait_ge(PE.h, trlast_pe[gb % 2])
                    d_op(v, lambda: v.tensor_scalar(stn[:], stm[:], str_[:], -1.0, ALU.mult, ALU.mult))
                    d_op(v, lambda: v.tensor_scalar(XNT[gb % 2][:], XB[gb % 2][:], str_[:], stn[:],
                                                    ALU.mult, ALU.add), dep=DV.n)
                    norm_dv[gb % 2] = DV.n
                block.vector(_nm)

            def emit_ln_tr(ti, b, q4):
                gb = ti * 2 + b
                nrm_here = norm_dv[gb % 2]
                for half in range(2):
                    views = []

                    def _tp(t, q4=q4, half=half, gb=gb, b=b, nrm_here=nrm_here, ti=ti):
                        t.wait_ge(DV.h, nrm_here)
                        if b == 0 and q4 == 0 and half == 0:
                            if ti > 0:
                                t.wait_ge(S[f"SXN{(ti - 1) % 2}"].h, xn_spill_sx[ti - 1])
                            t.wait_ge(PE.h, qkv_mm_last[ti - 1] if ti > 0 else 0)
                        for k in range(2):
                            c = q4 * 4 + half * 2 + k
                            view, n = emit_transpose(t, XNT[gb % 2][:, c * 128:(c + 1) * 128], pTR)
                            views.append((view, n, c))
                    block.tensor(_tp)

                    def _ev(v, views=views, b=b):
                        for view, n, c in views:
                            v.wait_ge(PE.h, tr_state["tr_pe"][n])
                            dst = XNF[:, c * TILE + b * 128: c * TILE + b * 128 + 128]
                            d_op(v, lambda view=view, dst=dst: v.tensor_copy(dst, view))
                            tr_state["ev_dv"].append(DV.n)
                    block.vector(_ev)
                if b == 1 and q4 == 7:
                    trlast_pe[0] = PE.n
                    trlast_pe[1] = PE.n

            def emit_xn_spill(ti):
                def _xs(sq, ti=ti):
                    sq.wait_ge(DV.h, DV.n)
                    dst = bass.AP(_th(xn_sp), ti * TILE,
                                  [[TOK, 128], [128 * TOK, ECH], [1, TILE]])
                    dma(sq, S[f"SXN{ti % 2}"], dst, XNF[:])
                block.sync(_xs)
                xn_spill_sx[ti] = S[f"SXN{ti % 2}"].n

            def emit_qkv_chunk(ti, ci):
                gi = ti * 24 + ci
                ln_evacs_dv = DV.n

                def _mm(t, ti=ti, ci=ci, gi=gi, ln_evacs_dv=ln_evacs_dv):
                    sem, cnt = wld[gi]
                    t.wait_ge(sem.h, cnt)
                    t.wait_ge(DV.h, ln_evacs_dv)
                    if gi >= 2:
                        t.wait_ge(DV.h, evac_dv[gi - 2])
                    ps = pQKV[gi % 2]
                    inst = None
                    for ec in range(ECH):
                        inst = t.matmul(ps[:, 0:TILE], WB[gi % 3][:, ec * 128:(ec + 1) * 128],
                                        XNF[:, ec * TILE:(ec + 1) * TILE],
                                        start=(ec == 0), stop=(ec == ECH - 1))
                    wmm_pe.append(p_mark(inst))
                block.tensor(_mm)
                if gi + 3 < NTILE * 24:
                    nxt = gi + 3
                    emit_w_load(nxt // 24, nxt % 24)

                def _evc(v, ti=ti, ci=ci, gi=gi):
                    v.wait_ge(PE.h, wmm_pe[gi])
                    if ci == 0:
                        v.wait_ge(PE.h, attn_done_pe[ti % 2])
                    m = ci % 8
                    ps = pQKV[gi % 2]
                    if ci < 8:
                        dst = bass.AP(_th(QG32[ti % 2]), m, [[TILE * 32, 128], [32, TILE]])
                        bias = bqg[:, m:m + 1]
                    elif ci < 16:
                        dst = bass.AP(_th(K32[ti % 2]), m, [[TILE * 32, 128], [32, TILE]])
                        bias = bks[:, m:m + 1]
                    else:
                        dst = bass.AP(_th(VA[ti % 2]), m, [[TILE * 9, 128], [9, TILE]])
                        bias = bvs[:, m:m + 1]
                    if gi == 0:
                        v.wait_ge(GM.h, gm_ready)
                    d_op(v, lambda: v.tensor_scalar(dst, ps[:, 0:TILE], bias, None, ALU.add))
                    evac_dv.append(DV.n)
                block.vector(_evc)
                if ci == 23:
                    qkv_mm_last[ti] = PE.n
                    qkv_ev_qk[ti] = evac_dv[ti * 24 + 15]
                    qkv_ev_v[ti] = evac_dv[ti * 24 + 23]

            def phaseA_units(ti):
                units = [lambda: emit_tile_head(ti)]
                for b in range(2):
                    units.append(lambda b=b: emit_ln_stats(ti, b))
                    for q4 in range(8):
                        units.append(lambda b=b, q4=q4: emit_ln_tr(ti, b, q4))
                units.append(lambda: emit_xn_spill(ti))
                for ci in range(24):
                    units.append(lambda ci=ci: emit_qkv_chunk(ti, ci))
                return units

            def emit_attn_group(ti, gidx):
                tb = gidx * GRP
                par = gidx % 2
                qg3, k3, va3 = QG32[ti % 2], K32[ti % 2], VA[ti % 2]

                def _tp(t, gidx=gidx, ti=ti):
                    t.wait_ge(DV.h, qkv_ev_qk[ti])
                    emit_transpose(t, qg3[:, gidx * 128:(gidx + 1) * 128], pTR)
                    emit_transpose(t, k3[:, gidx * 128:(gidx + 1) * 128], pTR)
                block.tensor(_tp)
                nq = tr_state["idx"] - 2

                def _ev(v, nq=nq, par=par):
                    for off, dstt in ((0, QGT[par]), (1, KTT[par])):
                        n = nq + off
                        v.wait_ge(PE.h, tr_state["tr_pe"][n])
                        v.wait_ge(PE.h, A["scl_pe"][par])
                        bank = pTR[n % 2]
                        slot = (n // 2) % 4
                        view = bank[:, slot * 128:(slot + 1) * 128]
                        d_op(v, lambda view=view, dstt=dstt: v.tensor_copy(dstt[:], view))
                        tr_state["ev_dv"].append(DV.n)
                block.vector(_ev)
                evk_dv = DV.n

                def _sc(t, par=par, evk_dv=evk_dv, exp_ac=A["exp_ac"], norm_g_dv=A["norm_dv"]):
                    t.wait_ge(DV.h, evk_dv)
                    t.wait_ge(ACS.h, exp_ac)
                    t.wait_ge(DV.h, norm_g_dv)
                    inst = None
                    for j in range(GRP):
                        inst = t.matmul(pSC[:, j * 512:j * 512 + 128],
                                        KTT[par][32 * j:32 * j + 8, :],
                                        QGT[par][32 * j:32 * j + 8, :],
                                        start=True, stop=True, tile_position=(32 * j, 0))
                    p_mark(inst)
                    A["scl_pe"][par] = PE.n
                block.tensor(_sc)
                sc_pe = PE.n

                def _ex(s, sc_pe=sc_pe, par=par):
                    s.wait_ge(PE.h, sc_pe)
                    s.wait_ge(PE.h, A["avl_pe"][par])
                    src = bass.AP(pSCh, 0, [[2048, 128], [512, 4], [1, 128]])
                    a_op(s, lambda: s.activation(ET[par][:], src, AF.Exp))
                block.scalar(_ex)
                A["exp_ac"] = ACS.n

                def _av(t, par=par, tb=tb, exp_ac=A["exp_ac"], norm_g_dv=A["norm_dv"], ti=ti,
                        va3=va3):
                    t.wait_ge(ACS.h, exp_ac)
                    t.wait_ge(DV.h, qkv_ev_v[ti])
                    t.wait_ge(DV.h, norm_g_dv)
                    inst = None
                    for j in range(GRP):
                        inst = t.matmul(pSC[:, 128 + j * 9:128 + (j + 1) * 9],
                                        ET[par][:, j * 128:(j + 1) * 128],
                                        va3[:, (tb + j) * 9:(tb + j + 1) * 9],
                                        start=True, stop=True)
                    p_mark(inst)
                    A["avl_pe"][par] = PE.n
                block.tensor(_av)
                av_pe = PE.n

                def _nr(v, av_pe=av_pe, tb=tb, ti=ti, gidx=gidx, par=par):
                    v.wait_ge(PE.h, av_pe)
                    if gidx == 0:
                        v.wait_ge(S[f"GSP{ti % 2}"].h, o8_spill_gs[ti % 2])
                    rb = REC[par]
                    d_op(v, lambda: v.reciprocal(rb[:], bass.AP(pSCh, 136, [[2048, 128], [9, 4]])))
                    outv = bass.AP(_th(O8N[ti % 2]), tb, [[8 * TILE, 128], [TILE, 8], [1, 4]])
                    inv = bass.AP(pSCh, 128, [[2048, 128], [1, 8], [9, 4]])
                    recv = bass.AP(_th(rb), 0, [[4, 128], [0, 8], [1, 4]])
                    d_op(v, lambda: v.tensor_tensor(outv, inv, recv, ALU.mult), dep=DV.n)
                block.vector(_nr)
                A["norm_dv"] = DV.n

            def emit_attn_spill(ti):
                attn_norm_dv = A["norm_dv"]

                def _as(gq, ti=ti, attn_norm_dv=attn_norm_dv):
                    gq.wait_ge(DV.h, attn_norm_dv)
                    for j in range(4):
                        dst = bass.AP(_th(attn_sp), j * TOK + ti * TILE,
                                      [[32 * TOK, 128], [4 * TOK, 8], [1, TILE]])
                        dma(gq, S[f"GSP{ti % 2}"], dst, O8N[ti % 2][:])
                block.gpsimd(_as)
                o8_spill_gs[ti % 2] = S[f"GSP{ti % 2}"].n

            for u in phaseA_units(0):
                u()
            for ti in range(NTILE):
                nxt = phaseA_units(ti + 1) if ti + 1 < NTILE else []
                k = 0
                for gidx in range(NGRP):
                    emit_attn_group(ti, gidx)
                    while k * NGRP < (gidx + 1) * len(nxt):
                        nxt[k]()
                        k += 1
                assert k == len(nxt)
                attn_done_pe[ti % 2] = PE.n
                emit_attn_spill(ti)

        barrier()

        with contextlib.ExitStack() as scB:
            eB = scB.enter_context
            XN3 = eB(nc.sbuf_tensor("XN3", [128, ECH * TOK], BF16))
            WUPB = [eB(nc.sbuf_tensor(f"WUPB{i}", [128, 4096], BF16)) for i in range(2)]
            WGTB = [eB(nc.sbuf_tensor(f"WGTB{i}", [128, 4096], BF16)) for i in range(2)]
            SG = [eB(nc.sbuf_tensor(f"SG{i}", [128, 512], F32)) for i in range(2)]
            HCH = [eB(nc.sbuf_tensor(f"HCH{i}", [128, TOK], BF16)) for i in range(2)]
            WDS = [eB(nc.sbuf_tensor(f"WDS{i}", [128, 4096], BF16)) for i in range(2)]
            pUP = [eB(nc.psum_tensor(f"pUP{i}", [128, 512], F32)) for i in range(2)]
            pGA = [eB(nc.psum_tensor(f"pGA{i}", [128, 512], F32)) for i in range(2)]

            def _x3(sq):
                src = bass.AP(_th(xn_sp), 0, [[TOK, 128], [128 * TOK, ECH], [1, TOK]])
                dma(sq, S["SH3"], XN3[:], src)
            block.sync(_x3)
            xn3_ld = S["SH3"].n

            upg_ld = []
            mm_up_pe = {}
            mm_ga_pe = {}
            silu_ac = {}
            mult_dv = {}
            hsp_sx = [0, 0]
            wd_ld_gp = [0] * FCH
            wd_st_sw = [0] * FCH

            def emit_upg_load(fc):
                def _w(gq, fc=fc):
                    if fc >= 2:
                        gq.wait_ge(PE.h, mm_ga_pe[(fc - 2, TG3 - 1)])
                    for wsrc, wdst in ((wupT, WUPB[fc % 2]), (wgT, WGTB[fc % 2])):
                        src = bass.AP(_th(wsrc), fc * 128,
                                      [[DFF, 128], [128 * DFF, ECH], [1, 128]])
                        upg_ld.append(dma(gq, S[f"GUP{fc % 2}"], wdst[:], src))
                block.gpsimd(_w)

            emit_upg_load(0)
            emit_upg_load(1)

            for fc in range(FCH):
                ld_cnt = upg_ld[fc * 2 + 1]
                for tg in range(TG3):
                    def _mm(t, fc=fc, tg=tg, ld_cnt=ld_cnt):
                        t.wait_ge(S[f"GUP{fc % 2}"].h, ld_cnt)
                        if fc == 0 and tg == 0:
                            t.wait_ge(S["SH3"].h, xn3_ld)
                        g2 = fc * TG3 + tg - 2
                        key2 = (g2 // TG3, g2 % TG3) if g2 >= 0 else None
                        if key2 in mult_dv:
                            t.wait_ge(DV.h, mult_dv[key2])
                        if key2 in silu_ac:
                            t.wait_ge(ACS.h, silu_ac[key2])
                        inst = None
                        for ec in range(ECH):
                            rhs = XN3[:, ec * TOK + tg * 512: ec * TOK + tg * 512 + 512]
                            inst = t.matmul(pUP[tg % 2][:], WUPB[fc % 2][:, ec * 128:(ec + 1) * 128],
                                            rhs, start=(ec == 0), stop=(ec == ECH - 1))
                        mm_up_pe[(fc, tg)] = p_mark(inst)
                        inst = None
                        for ec in range(ECH):
                            rhs = XN3[:, ec * TOK + tg * 512: ec * TOK + tg * 512 + 512]
                            inst = t.matmul(pGA[tg % 2][:], WGTB[fc % 2][:, ec * 128:(ec + 1) * 128],
                                            rhs, start=(ec == 0), stop=(ec == ECH - 1))
                        mm_ga_pe[(fc, tg)] = p_mark(inst)
                    block.tensor(_mm)

                    def _si(s, fc=fc, tg=tg):
                        s.wait_ge(PE.h, mm_ga_pe[(fc, tg)])
                        g2 = fc * TG3 + tg - 2
                        key2 = (g2 // TG3, g2 % TG3) if g2 >= 0 else None
                        if key2 in mult_dv:
                            s.wait_ge(DV.h, mult_dv[key2])
                        silu_ac[(fc, tg)] = a_op(
                            s, lambda: s.activation(SG[tg % 2][:], pGA[tg % 2][:], AF.Silu))
                    block.scalar(_si)

                    def _ml(v, fc=fc, tg=tg):
                        v.wait_ge(PE.h, mm_up_pe[(fc, tg)])
                        v.wait_ge(ACS.h, silu_ac[(fc, tg)])
                        if tg == 0:
                            v.wait_ge(S[f"SH{fc % 2}"].h, hsp_sx[fc % 2])
                        dst = HCH[fc % 2][:, tg * 512:(tg + 1) * 512]
                        mult_dv[(fc, tg)] = d_op(
                            v, lambda: v.tensor_tensor(dst, pUP[tg % 2][:], SG[tg % 2][:], ALU.mult))
                    block.vector(_ml)

                def _hs(sq, fc=fc):
                    sq.wait_ge(DV.h, mult_dv[(fc, TG3 - 1)])
                    dst = bass.AP(_th(h_sp), fc * 128 * TOK, [[TOK, 128], [1, TOK]])
                    dma(sq, S[f"SH{fc % 2}"], dst, HCH[fc % 2][:])
                block.sync(_hs)
                hsp_sx[fc % 2] = S[f"SH{fc % 2}"].n

                if fc + 2 < FCH:
                    emit_upg_load(fc + 2)

                def _wdl(gq, fc=fc):
                    if fc >= 2:
                        gq.wait_ge(S[f"SWD{fc % 2}"].h, wd_st_sw[fc - 2])
                    src = bass.AP(_th(wdT), fc * 128 * EMB, [[EMB, 128], [1, EMB]])
                    wd_ld_gp[fc] = dma(gq, S[f"GWD{fc % 2}"], WDS[fc % 2][:], src)
                block.gpsimd(_wdl)

                def _wds(sq, fc=fc):
                    sq.wait_ge(S[f"GWD{fc % 2}"].h, wd_ld_gp[fc])
                    dst = bass.AP(_th(wd_bf), fc * 128 * EMB, [[EMB, 128], [1, EMB]])
                    wd_st_sw[fc] = dma(sq, S[f"SWD{fc % 2}"], dst, WDS[fc % 2][:])
                block.sync(_wds)

        barrier()

        with contextlib.ExitStack() as scC:
            eC = scC.enter_context
            H3B = eC(nc.sbuf_tensor("H3B", [128, FCH * 512], BF16))
            WDB = [eC(nc.sbuf_tensor(f"WDB{i}", [128, FCH * 128], BF16)) for i in range(2)]
            YB = [eC(nc.sbuf_tensor(f"YB{i}", [128, 512], F32)) for i in range(2)]
            pY = [eC(nc.psum_tensor(f"pY{i}", [128, 512], F32)) for i in range(2)]

            ymm_pe = {}
            yev_dv = {}
            ysp_sx = {}
            h3_ld = [0] * SUB
            wd_ld = {}

            def emit_wd_load(sub, dm):
                def _w(sq, sub=sub, dm=dm):
                    gi = sub * DMCH + dm
                    if gi >= 2:
                        pgi = gi - 2
                        sq.wait_ge(PE.h, ymm_pe[(pgi // DMCH, pgi % DMCH)])
                    src = bass.AP(_th(wd_bf), dm * 128,
                                  [[EMB, 128], [128 * EMB, FCH], [1, 128]])
                    wd_ld[(sub, dm)] = dma(sq, S[f"SWB{gi % 2}"], WDB[gi % 2][:], src)
                block.sync(_w)

            FH = 43
            for sub in range(SUB):
                def _h3(sq, sub=sub):
                    if sub >= 1:
                        sq.wait_ge(PE.h, ymm_pe[(sub - 1, DMCH - 1)])
                    srcA = bass.AP(_th(h_sp), sub * 512,
                                   [[TOK, 128], [128 * TOK, FH], [1, 512]])
                    ca = dma(sq, S["SH3"], H3B[:, 0:FH * 512], srcA)
                    srcB = bass.AP(_th(h_sp), FH * 128 * TOK + sub * 512,
                                   [[TOK, 128], [128 * TOK, FCH - FH], [1, 512]])
                    cb = dma(sq, S["SH3B"], H3B[:, FH * 512:], srcB)
                    h3_ld[sub] = (ca, cb)
                block.sync(_h3)
                if sub == 0:
                    emit_wd_load(0, 0)
                    emit_wd_load(0, 1)

                for dm in range(DMCH):
                    gi = sub * DMCH + dm

                    def _mm(t, sub=sub, dm=dm, gi=gi):
                        t.wait_ge(S[f"SWB{gi % 2}"].h, wd_ld[(sub, dm)])
                        t.wait_ge(S["SH3"].h, h3_ld[sub][0])
                        key2 = (sub, dm - 2) if dm >= 2 else (sub - 1, dm + DMCH - 2)
                        if key2 in yev_dv:
                            t.wait_ge(DV.h, yev_dv[key2])
                        inst = None
                        for fcc in range(FH):
                            inst = t.matmul(pY[gi % 2][:], WDB[gi % 2][:, fcc * 128:(fcc + 1) * 128],
                                            H3B[:, fcc * 512:(fcc + 1) * 512],
                                            start=(fcc == 0), stop=False)
                        t.wait_ge(S["SH3B"].h, h3_ld[sub][1])
                        for fcc in range(FH, FCH):
                            inst = t.matmul(pY[gi % 2][:], WDB[gi % 2][:, fcc * 128:(fcc + 1) * 128],
                                            H3B[:, fcc * 512:(fcc + 1) * 512],
                                            start=False, stop=(fcc == FCH - 1))
                        ymm_pe[(sub, dm)] = p_mark(inst)
                    block.tensor(_mm)
                    ngi = gi + 2
                    if ngi < SUB * DMCH:
                        emit_wd_load(ngi // DMCH, ngi % DMCH)

                    def _ev(v, sub=sub, dm=dm, gi=gi):
                        v.wait_ge(PE.h, ymm_pe[(sub, dm)])
                        key2 = (sub, dm - 2) if dm >= 2 else (sub - 1, dm + DMCH - 2)
                        if key2 in ysp_sx:
                            v.wait_ge(S[f"SY{(key2[0] * DMCH + key2[1]) % 2}"].h, ysp_sx[key2])
                        yev_dv[(sub, dm)] = d_op(
                            v, lambda: v.tensor_copy(YB[gi % 2][:], pY[gi % 2][:]))
                    block.vector(_ev)

                    def _ys(gq, sub=sub, dm=dm, gi=gi):
                        gq.wait_ge(DV.h, yev_dv[(sub, dm)])
                        dst = bass.AP(_th(y_sp), dm * 128 * TOK + sub * 512,
                                      [[TOK, 128], [1, 512]])
                        ysp_sx[(sub, dm)] = dma(gq, S[f"SY{gi % 2}"], dst, YB[gi % 2][:])
                    block.gpsimd(_ys)

        barrier()

        with contextlib.ExitStack() as scD:
            eD = scD.enter_context
            YL = [eD(nc.sbuf_tensor(f"YL{i}", [128, EMB], F32)) for i in range(2)]
            AL = [eD(nc.sbuf_tensor(f"AL{i}", [128, EMB], F32)) for i in range(2)]
            XL = [eD(nc.sbuf_tensor(f"XL{i}", [128, EMB], F32)) for i in range(2)]
            SFM = eD(nc.sbuf_tensor("SFM", [128, EMB], F32))
            OT = [eD(nc.sbuf_tensor(f"OT{i}", [128, EMB], F32)) for i in range(2)]
            pTC = [eD(nc.psum_tensor(f"pTC{i}", [128, 512], F32)) for i in range(2)]

            add_dv = [0] * (TOK // 128)
            trc_pe = {}
            oadd_dv = {}
            out_g = [0, 0]
            sfm_tr_pe = 0

            for blk in range(TOK // 128):
                def _ld(sq, blk=blk):
                    if blk >= 2:
                        sq.wait_ge(DV.h, add_dv[blk - 2])
                        sq.wait_ge(S[f"GOUT{blk % 2}"].h, out_g[blk % 2])
                    ysrc = bass.AP(_th(y_sp), blk * 128,
                                   [[TOK, 128], [128 * TOK, ECH], [1, 128]])
                    dma(sq, S[f"SCL{blk % 2}"], YL[blk % 2][:], ysrc)
                    asrc = bass.AP(_th(attn_sp), blk * 128,
                                   [[TOK, 128], [128 * TOK, ECH], [1, 128]])
                    dma(sq, S[f"SCL{blk % 2}"], AL[blk % 2][:], asrc)
                    xsrc = bass.AP(_th(x_in), blk * 128 * EMB, [[EMB, 128], [1, EMB]])
                    dma(sq, S[f"SCL{blk % 2}"], XL[blk % 2][:], xsrc)
                block.sync(_ld)
                here = S[f"SCL{blk % 2}"].n

                def _ad(v, blk=blk, here=here, sfm_tr_pe=sfm_tr_pe):
                    v.wait_ge(S[f"SCL{blk % 2}"].h, here)
                    if blk >= 1:
                        v.wait_ge(PE.h, sfm_tr_pe)
                    add_dv[blk] = d_op(
                        v, lambda: v.tensor_tensor(SFM[:], YL[blk % 2][:], AL[blk % 2][:], ALU.add))
                block.vector(_ad)

                for q2 in range(16):
                    def _tp(t, blk=blk, q2=q2):
                        t.wait_ge(DV.h, add_dv[blk])
                        for k in range(2):
                            c = q2 * 2 + k
                            n = blk * 32 + c
                            bank = pTC[n % 2]
                            slot = (n // 2) % 4
                            if n >= 2:
                                t.wait_ge(DV.h, oadd_dv.get(n - 2, 0))
                            inst = t.transpose(bank[:, slot * 128:(slot + 1) * 128],
                                               SFM[:, c * 128:(c + 1) * 128], idf[:])
                            trc_pe[n] = p_mark(inst)
                    block.tensor(_tp)

                    def _oa(v, blk=blk, q2=q2):
                        for k in range(2):
                            c = q2 * 2 + k
                            n = blk * 32 + c
                            if c == 0:
                                v.wait_ge(S[f"GOUT{blk % 2}"].h, out_g[blk % 2])
                            v.wait_ge(PE.h, trc_pe[n])
                            bank = pTC[n % 2]
                            slot = (n // 2) % 4
                            view = bank[:, slot * 128:(slot + 1) * 128]
                            dst = OT[blk % 2][:, c * 128:(c + 1) * 128]
                            xsl = XL[blk % 2][:, c * 128:(c + 1) * 128]
                            oadd_dv[n] = d_op(
                                v, lambda view=view, dst=dst, xsl=xsl:
                                v.tensor_tensor(dst, view, xsl, ALU.add))
                    block.vector(_oa)
                sfm_tr_pe = PE.n

                def _ow(gq, blk=blk):
                    gq.wait_ge(DV.h, oadd_dv[blk * 32 + 31])
                    dst = bass.AP(_th(out_h), blk * 128 * EMB, [[EMB, 128], [1, EMB]])
                    dma(gq, S[f"GOUT{blk % 2}"], dst, OT[blk % 2][:])
                block.gpsimd(_ow)
                out_g[blk % 2] = S[f"GOUT{blk % 2}"].n

            def _fin(gq):
                gq.wait_ge(S["GOUT0"].h, S["GOUT0"].n)
                gq.wait_ge(S["GOUT1"].h, S["GOUT1"].n)
            block.gpsimd(_fin)

    return nc


def kernel(L0_attn_x, L0_attn_Wq, L0_attn_bq, L0_attn_Wk, L0_attn_bk,
           L0_attn_Wv, L0_attn_bv, L0_ffn_W_gate, L0_ffn_W_up, L0_ffn_W_down):
    x = np.asarray(L0_attn_x, dtype=np.float32).reshape(16384, 4096)
    wq = np.asarray(L0_attn_Wq, dtype=np.float32)
    wk = np.asarray(L0_attn_Wk, dtype=np.float32)
    wv = np.asarray(L0_attn_Wv, dtype=np.float32)
    shared = {
        "wq_r": np.ascontiguousarray(np.transpose(wq, (0, 2, 1)).reshape(4096, 4096)),
        "wk_r": np.ascontiguousarray(np.transpose(wk, (0, 2, 1)).reshape(4096, 1024)),
        "wv_r": np.ascontiguousarray(np.transpose(wv, (0, 2, 1)).reshape(4096, 1024)),
        "bq": np.ascontiguousarray(np.asarray(L0_attn_bq, dtype=np.float32)),
        "bk": np.ascontiguousarray(np.asarray(L0_attn_bk, dtype=np.float32)),
        "bv": np.ascontiguousarray(np.asarray(L0_attn_bv, dtype=np.float32)),
        "wupT": np.ascontiguousarray(np.asarray(L0_ffn_W_up, dtype=np.float32).T),
        "wgT": np.ascontiguousarray(np.asarray(L0_ffn_W_gate, dtype=np.float32).T),
        "wdT": np.ascontiguousarray(np.asarray(L0_ffn_W_down, dtype=np.float32).T),
        "idn": np.eye(128, dtype=np.float32),
    }
    nc = build_nc()
    in_maps = []
    for c in range(8):
        m = dict(shared)
        m["x"] = np.ascontiguousarray(x[c * TOK:(c + 1) * TOK])
        in_maps.append(m)
    res = run_bass_kernel_spmd(nc, in_maps, core_ids=list(range(8)))
    out = np.concatenate([r["out"] for r in res.results], axis=0)
    return out.reshape(1, 16384, 4096)


# version 07
# speedup vs baseline: 1.0847x; 1.0581x over previous
import numpy as np
import contextlib
import concourse.bass as bass
import concourse.mybir as mybir
from concourse.bass_utils import run_bass_kernel_spmd

F32 = mybir.dt.float32
BF16 = mybir.dt.bfloat16
AF = mybir.ActivationFunctionType
ALU = mybir.AluOpType
AX = mybir.AxisListType

TOK = 2048
EMB = 4096
ECH = 32
DFF = 11008
FCH = 86
TILE = 256
NTILE = TOK // TILE
BLK = 128
NBLK = TOK // BLK
GRP = 2
NGRP = TILE // GRP
TG3 = 4
SUB = 4
DMCH = 32
INV_SQRT_D = 0.08838834764831845
INV_EMB = 1.0 / 4096.0
LN_EPS = 1e-5


def _th(x):
    return x.tensor if hasattr(x, "tensor") else x


class _Ctr:
    __slots__ = ("h", "n")

    def __init__(self, h):
        self.h = h
        self.n = 0


def build_nc():
    nc = bass.Bass("TRN2", target_bir_lowering=False)

    x_in = nc.dram_tensor("x", [TOK, EMB], F32, kind="ExternalInput")
    wq_r = nc.dram_tensor("wq_r", [EMB, 4096], F32, kind="ExternalInput")
    wk_r = nc.dram_tensor("wk_r", [EMB, 1024], F32, kind="ExternalInput")
    wv_r = nc.dram_tensor("wv_r", [EMB, 1024], F32, kind="ExternalInput")
    bq_in = nc.dram_tensor("bq", [128, 32], F32, kind="ExternalInput")
    bk_in = nc.dram_tensor("bk", [128, 8], F32, kind="ExternalInput")
    bv_in = nc.dram_tensor("bv", [128, 8], F32, kind="ExternalInput")
    wupT = nc.dram_tensor("wupT", [EMB, DFF], F32, kind="ExternalInput")
    wgT = nc.dram_tensor("wgT", [EMB, DFF], F32, kind="ExternalInput")
    wdT = nc.dram_tensor("wdT", [DFF, EMB], F32, kind="ExternalInput")
    idn = nc.dram_tensor("idn", [128, 128], F32, kind="ExternalInput")
    out_h = nc.dram_tensor("out", [TOK, EMB], F32, kind="ExternalOutput")

    wqg_bf = nc.dram_tensor("wqg_bf", [EMB, 1024], BF16, kind="Internal")
    wd_bf = nc.dram_tensor("wd_bf", [DFF, EMB], BF16, kind="Internal")
    xn_sp = nc.dram_tensor("xn_sp", [EMB, TOK], BF16, kind="Internal")
    attn_sp = nc.dram_tensor("attn_sp", [EMB, TOK], F32, kind="Internal")
    h_sp = nc.dram_tensor("h_sp", [DFF, TOK], BF16, kind="Internal")
    y_sp = nc.dram_tensor("y_sp", [EMB, TOK], F32, kind="Internal")

    g = contextlib.ExitStack()
    with g:
        es = g.enter_context
        idb = es(nc.sbuf_tensor("idb", [128, 128], BF16))
        idf = es(nc.sbuf_tensor("idf", [128, 128], F32))
        bqs = es(nc.sbuf_tensor("bqs", [128, 32], F32))
        bks = es(nc.sbuf_tensor("bks", [128, 8], F32))
        bvs = es(nc.sbuf_tensor("bvs", [128, 8], F32))
        bqg = es(nc.sbuf_tensor("bqg", [128, 8], F32))
        bt1 = es(nc.sbuf_tensor("bt1", [128, 8], F32))
        bt2 = es(nc.sbuf_tensor("bt2", [128, 8], F32))

        names = (["DV", "AC", "PE", "GM"]
                 + ["SX0", "SX1", "SW0", "SW1", "SW2", "SXN0", "SXN1",
                    "GKV0", "GKV1", "GKV2", "GSP0", "GSP1"]
                 + ["SQS0", "SQS1", "SQW0", "SQW1", "SIN", "GIN"]
                 + ["GUP0", "GUP1", "SH0", "SH1", "GWD0", "GWD1", "SWD0", "SWD1"]
                 + ["SH3", "SH3B", "SWB0", "SWB1", "SY0", "SY1"]
                 + ["SCL0", "SCL1", "GOUT0", "GOUT1"])
        S = {n: _Ctr(es(nc.semaphore(n))) for n in names}
        DV, ACS, PE, GM = S["DV"], S["AC"], S["PE"], S["GM"]

        block = es(nc.Block())

        def d_op(v, fn, dep=None):
            if dep is not None:
                v.wait_ge(DV.h, dep)
            fn().then_inc(DV.h, 1)
            DV.n += 1
            return DV.n

        def a_op(s, fn):
            s.wait_ge(ACS.h, ACS.n)
            fn().then_inc(ACS.h, 1)
            ACS.n += 1
            return ACS.n

        def p_mark(inst):
            inst.then_inc(PE.h, 1)
            PE.n += 1
            return PE.n

        def dma(q, sem, out, in_):
            q.dma_start(out=out, in_=in_).then_inc(sem.h, 16)
            sem.n += 16
            return sem.n

        def barrier():
            cnts = [(c.h, c.n) for c in S.values()]
            for engname in ("tensor", "vector", "scalar", "sync", "gpsimd"):
                def _mk(e):
                    for h, n in cnts:
                        e.wait_ge(h, n)
                getattr(block, engname)(_mk)

        tr_state = {"idx": 0, "ev_dv": [], "tr_pe": []}

        def emit_transpose(t, src, psums):
            n = tr_state["idx"]
            bank = psums[n % 2]
            slot = (n // 2) % 4
            rows = src.shape[-1]
            view = bank[0:rows, slot * 128:(slot + 1) * 128]
            if n >= 2:
                t.wait_ge(DV.h, tr_state["ev_dv"][n - 2])
            inst = t.transpose(view, src, idb[:])
            p_mark(inst)
            tr_state["tr_pe"].append(PE.n)
            tr_state["idx"] += 1
            return view, n

        with contextlib.ExitStack() as sc0:
            e0 = sc0.enter_context
            WQS = [e0(nc.sbuf_tensor(f"WQS{i}", [128, 4096], F32)) for i in range(2)]
            s01 = e0(nc.sbuf_tensor("s01", [128, 1024], F32))
            s23 = e0(nc.sbuf_tensor("s23", [128, 1024], F32))
            ssum = e0(nc.sbuf_tensor("ssum", [128, 1024], F32))
            wqg_t = [e0(nc.sbuf_tensor(f"wqgt{i}", [128, 1024], BF16)) for i in range(2)]

            def _gp(gq):
                dma(gq, S["GIN"], idb[:], idn.ap())
            block.gpsimd(_gp)

            def _sy(sq):
                dma(sq, S["SIN"], idf[:], idn.ap())
                dma(sq, S["SIN"], bqs[:], bq_in.ap())
                dma(sq, S["SIN"], bks[:], bk_in.ap())
                dma(sq, S["SIN"], bvs[:], bv_in.ap())
            block.sync(_sy)

            def _dv(v):
                v.wait_ge(S["SIN"].h, S["SIN"].n)
                v.wait_ge(S["GIN"].h, S["GIN"].n)
                bqh = _th(bqs)
                views = [bass.AP(bqh, j, [[32, 128], [4, 8]]) for j in range(4)]
                d_op(v, lambda: v.tensor_tensor(bt1[:], views[0], views[1], ALU.add))
                d_op(v, lambda: v.tensor_tensor(bt2[:], views[2], views[3], ALU.add))
                d_op(v, lambda: v.tensor_tensor(bt1[:], bt1[:], bt2[:], ALU.add), dep=DV.n)
                d_op(v, lambda: v.tensor_scalar(bqg[:], bt1[:], INV_SQRT_D, None, ALU.mult), dep=DV.n)
            block.vector(_dv)

            s23_dv = [0] * ECH
            st_sq = [0] * ECH
            ld_sq = [0] * ECH
            for ec in range(ECH):
                buf = ec % 2

                def _ld(sq, ec=ec, buf=buf):
                    if ec >= 2:
                        sq.wait_ge(DV.h, s23_dv[ec - 2])
                    src = bass.AP(_th(wq_r), ec * 128 * 4096, [[4096, 128], [1, 4096]])
                    ld_sq[ec] = dma(sq, S[f"SQS{buf}"], WQS[buf][:], src)
                block.sync(_ld)

                def _gs(v, ec=ec, buf=buf):
                    v.wait_ge(S[f"SQS{buf}"].h, ld_sq[ec])
                    if ec >= 2:
                        v.wait_ge(S[f"SQW{buf}"].h, st_sq[ec - 2])
                    wh = _th(WQS[buf])
                    vw = [bass.AP(wh, j * 128, [[4096, 128], [512, 8], [1, 128]]) for j in range(4)]
                    d_op(v, lambda: v.tensor_tensor(s01[:], vw[0], vw[1], ALU.add))
                    d_op(v, lambda: v.tensor_tensor(s23[:], vw[2], vw[3], ALU.add))
                    s23_dv[ec] = DV.n
                    d_op(v, lambda: v.tensor_tensor(ssum[:], s01[:], s23[:], ALU.add), dep=DV.n)
                    d_op(v, lambda: v.tensor_scalar(wqg_t[buf][:], ssum[:], INV_SQRT_D, None, ALU.mult), dep=DV.n)
                block.vector(_gs)

                def _st(sq, ec=ec, buf=buf):
                    sq.wait_ge(DV.h, DV.n)
                    dst = bass.AP(_th(wqg_bf), ec * 128 * 1024, [[1024, 128], [1, 1024]])
                    st_sq[ec] = dma(sq, S[f"SQW{buf}"], dst, wqg_t[buf][:])
                block.sync(_st)

        barrier()

        with contextlib.ExitStack() as scA:
            eA = scA.enter_context
            XB = [eA(nc.sbuf_tensor(f"XB{i}", [128, EMB], F32)) for i in range(2)]
            XSQ = eA(nc.sbuf_tensor("XSQ", [128, EMB], F32))
            XNT = [eA(nc.sbuf_tensor(f"XNT{i}", [128, EMB], BF16)) for i in range(2)]
            XNF = eA(nc.sbuf_tensor("XNF", [128, ECH * TILE], BF16))
            WB = [eA(nc.sbuf_tensor(f"WB{i}", [128, 4096], BF16)) for i in range(3)]
            QG32 = [eA(nc.sbuf_tensor(f"QG32{i}", [128, TILE * 32], BF16)) for i in range(2)]
            K32 = [eA(nc.sbuf_tensor(f"K32{i}", [128, TILE * 32], BF16)) for i in range(2)]
            VA = [eA(nc.sbuf_tensor(f"VA{i}", [128, TILE * 9], BF16)) for i in range(2)]
            QGT = [eA(nc.sbuf_tensor(f"QGT{i}", [128, 128], BF16)) for i in range(2)]
            KTT = [eA(nc.sbuf_tensor(f"KTT{i}", [128, 128], BF16)) for i in range(2)]
            ET = [eA(nc.sbuf_tensor(f"ET{i}", [128, 256], BF16)) for i in range(2)]
            O8N = [eA(nc.sbuf_tensor(f"O8N{i}", [128, 8 * TILE], F32)) for i in range(2)]
            REC = [eA(nc.sbuf_tensor(f"REC{i}", [128, 2], F32)) for i in range(2)]
            st1 = eA(nc.sbuf_tensor("st1", [128, 1], F32))
            st2 = eA(nc.sbuf_tensor("st2", [128, 1], F32))
            stm = eA(nc.sbuf_tensor("stm", [128, 1], F32))
            stv = eA(nc.sbuf_tensor("stv", [128, 1], F32))
            str_ = eA(nc.sbuf_tensor("str", [128, 1], F32))
            stl = eA(nc.sbuf_tensor("stl", [128, 1], F32))
            stn = eA(nc.sbuf_tensor("stn", [128, 1], F32))
            pQKV = [eA(nc.psum_tensor(f"pQKV{i}", [128, 512], F32)) for i in range(2)]
            pTR = [eA(nc.psum_tensor(f"pTR{i}", [128, 512], BF16)) for i in range(2)]
            pSC = [eA(nc.psum_tensor(f"pSC{i}", [128, 1024], F32)) for i in range(2)]
            pSCh = [_th(p) for p in pSC]

            def _ini(gq):
                for i in range(2):
                    gq.memset(QG32[i][:], 0.0).then_inc(GM.h, 1); GM.n += 1
                    gq.memset(K32[i][:], 0.0).then_inc(GM.h, 1); GM.n += 1
                    gq.memset(bass.AP(_th(VA[i]), 8, [[TILE * 9, 128], [9, TILE]]),
                              1.0).then_inc(GM.h, 1)
                    GM.n += 1
            block.gpsimd(_ini)
            gm_ready = GM.n

            norm_dv = [0, 0]
            x_ld = [0] * (2 * NTILE + 2)
            trlast_pe = [0, 0]
            xn_spill_sx = [0] * (NTILE + 1)
            qkv_mm_last = [0] * (NTILE + 1)
            qkv_ev_qk = [0] * (NTILE + 1)
            qkv_ev_v = [0] * (NTILE + 1)
            attn_done_pe = [0, 0]
            o8_spill_gs = [0, 0]
            wld = []
            wmm_pe = []
            evac_dv = []
            A = {"exp_ac": [0, 0], "norm_dv": [0, 0], "scl_pe": [0, 0],
                 "avl_pe": [0, 0], "evk_dv": [0, 0]}

            def emit_w_load(ti, ci):
                gi = ti * 24 + ci
                m = ci % 8
                if ci < 8:
                    eng = "sync"
                    sem = S[f"SW{gi % 3}"]
                    src = bass.AP(_th(wqg_bf), m * 128,
                                  [[1024, 128], [128 * 1024, 32], [1, 128]])
                else:
                    eng = "gpsimd"
                    sem = S[f"GKV{gi % 3}"]
                    src = bass.AP(_th(wk_r if ci < 16 else wv_r), m * 128,
                                  [[1024, 128], [128 * 1024, 32], [1, 128]])

                def _w(q, gi=gi, sem=sem, src=src):
                    if gi >= 3:
                        q.wait_ge(PE.h, wmm_pe[gi - 3])
                    wld.append((sem, dma(q, sem, WB[gi % 3][:], src)))
                getattr(block, eng)(_w)

            def emit_tile_head(ti):
                def _xl(sq, ti=ti):
                    for b in range(2):
                        gb = ti * 2 + b
                        if gb >= 2:
                            sq.wait_ge(DV.h, norm_dv[gb % 2])
                        src = bass.AP(_th(x_in), gb * 128 * EMB, [[EMB, 128], [1, EMB]])
                        x_ld[gb] = dma(sq, S[f"SX{gb % 2}"], XB[gb % 2][:], src)
                block.sync(_xl)
                if ti == 0:
                    for ci in range(3):
                        emit_w_load(0, ci)

            def emit_ln_stats(ti, b):
                gb = ti * 2 + b

                def _st(v, gb=gb):
                    v.wait_ge(S[f"SX{gb % 2}"].h, x_ld[gb])
                    xb = XB[gb % 2]
                    d_op(v, lambda: v.reduce_sum(st1[:], xb[:], AX.X))
                    d_op(v, lambda: v.tensor_tensor(XSQ[:], xb[:], xb[:], ALU.mult))
                    d_op(v, lambda: v.reduce_sum(st2[:], XSQ[:], AX.X), dep=DV.n)
                    d_op(v, lambda: v.tensor_scalar(stm[:], st1[:], INV_EMB, None, ALU.mult), dep=DV.n)
                    d_op(v, lambda: v.tensor_scalar(st2[:], st2[:], INV_EMB, None, ALU.mult), dep=DV.n)
                    d_op(v, lambda: v.tensor_tensor(st1[:], stm[:], stm[:], ALU.mult), dep=DV.n)
                    d_op(v, lambda: v.tensor_tensor(stv[:], st2[:], st1[:], ALU.subtract), dep=DV.n)
                    d_op(v, lambda: v.tensor_scalar(stv[:], stv[:], LN_EPS, None, ALU.add), dep=DV.n)
                block.vector(_st)
                var_dv = DV.n

                def _l1(s, var_dv=var_dv):
                    s.wait_ge(DV.h, var_dv)
                    a_op(s, lambda: s.activation(stl[:], stv[:], AF.Ln))
                block.scalar(_l1)
                ln_ac = ACS.n

                def _l2(v, ln_ac=ln_ac):
                    v.wait_ge(ACS.h, ln_ac)
                    d_op(v, lambda: v.tensor_scalar(stl[:], stl[:], -0.5, None, ALU.mult))
                block.vector(_l2)
                lh_dv = DV.n

                def _l3(s, lh_dv=lh_dv):
                    s.wait_ge(DV.h, lh_dv)
                    a_op(s, lambda: s.activation(str_[:], stl[:], AF.Exp))
                block.scalar(_l3)
                rstd_ac = ACS.n

                def _nm(v, gb=gb, rstd_ac=rstd_ac):
                    v.wait_ge(ACS.h, rstd_ac)
                    if gb >= 2:
                        v.wait_ge(PE.h, trlast_pe[gb % 2])
                    d_op(v, lambda: v.tensor_scalar(stn[:], stm[:], str_[:], -1.0, ALU.mult, ALU.mult))
                    d_op(v, lambda: v.tensor_scalar(XNT[gb % 2][:], XB[gb % 2][:], str_[:], stn[:],
                                                    ALU.mult, ALU.add), dep=DV.n)
                    norm_dv[gb % 2] = DV.n
                block.vector(_nm)

            def emit_ln_tr(ti, b, q4):
                gb = ti * 2 + b
                nrm_here = norm_dv[gb % 2]
                for half in range(2):
                    views = []

                    def _tp(t, q4=q4, half=half, gb=gb, b=b, nrm_here=nrm_here, ti=ti):
                        t.wait_ge(DV.h, nrm_here)
                        if b == 0 and q4 == 0 and half == 0:
                            if ti > 0:
                                t.wait_ge(S[f"SXN{(ti - 1) % 2}"].h, xn_spill_sx[ti - 1])
                            t.wait_ge(PE.h, qkv_mm_last[ti - 1] if ti > 0 else 0)
                        for k in range(2):
                            c = q4 * 4 + half * 2 + k
                            view, n = emit_transpose(t, XNT[gb % 2][:, c * 128:(c + 1) * 128], pTR)
                            views.append((view, n, c))
                    block.tensor(_tp)

                    def _ev(v, views=views, b=b):
                        for view, n, c in views:
                            v.wait_ge(PE.h, tr_state["tr_pe"][n])
                            dst = XNF[:, c * TILE + b * 128: c * TILE + b * 128 + 128]
                            d_op(v, lambda view=view, dst=dst: v.tensor_copy(dst, view))
                            tr_state["ev_dv"].append(DV.n)
                    block.vector(_ev)
                if b == 1 and q4 == 7:
                    trlast_pe[0] = PE.n
                    trlast_pe[1] = PE.n

            def emit_xn_spill(ti):
                def _xs(sq, ti=ti):
                    sq.wait_ge(DV.h, DV.n)
                    dst = bass.AP(_th(xn_sp), ti * TILE,
                                  [[TOK, 128], [128 * TOK, ECH], [1, TILE]])
                    dma(sq, S[f"SXN{ti % 2}"], dst, XNF[:])
                block.sync(_xs)
                xn_spill_sx[ti] = S[f"SXN{ti % 2}"].n

            def emit_qkv_chunk(ti, ci):
                gi = ti * 24 + ci
                ln_evacs_dv = DV.n

                def _mm(t, ti=ti, ci=ci, gi=gi, ln_evacs_dv=ln_evacs_dv):
                    sem, cnt = wld[gi]
                    t.wait_ge(sem.h, cnt)
                    t.wait_ge(DV.h, ln_evacs_dv)
                    if gi >= 2:
                        t.wait_ge(DV.h, evac_dv[gi - 2])
                    ps = pQKV[gi % 2]
                    inst = None
                    for ec in range(ECH):
                        inst = t.matmul(ps[:, 0:TILE], WB[gi % 3][:, ec * 128:(ec + 1) * 128],
                                        XNF[:, ec * TILE:(ec + 1) * TILE],
                                        start=(ec == 0), stop=(ec == ECH - 1))
                    wmm_pe.append(p_mark(inst))
                block.tensor(_mm)
                if gi + 3 < NTILE * 24:
                    nxt = gi + 3
                    emit_w_load(nxt // 24, nxt % 24)

                def _evc(v, ti=ti, ci=ci, gi=gi):
                    v.wait_ge(PE.h, wmm_pe[gi])
                    if ci == 0:
                        v.wait_ge(PE.h, attn_done_pe[ti % 2])
                    m = ci % 8
                    ps = pQKV[gi % 2]
                    if ci < 8:
                        dst = bass.AP(_th(QG32[ti % 2]), m, [[TILE * 32, 128], [32, TILE]])
                        bias = bqg[:, m:m + 1]
                    elif ci < 16:
                        dst = bass.AP(_th(K32[ti % 2]), m, [[TILE * 32, 128], [32, TILE]])
                        bias = bks[:, m:m + 1]
                    else:
                        dst = bass.AP(_th(VA[ti % 2]), m, [[TILE * 9, 128], [9, TILE]])
                        bias = bvs[:, m:m + 1]
                    if gi == 0:
                        v.wait_ge(GM.h, gm_ready)
                    d_op(v, lambda: v.tensor_scalar(dst, ps[:, 0:TILE], bias, None, ALU.add))
                    evac_dv.append(DV.n)
                block.vector(_evc)
                if ci == 23:
                    qkv_mm_last[ti] = PE.n
                    qkv_ev_qk[ti] = evac_dv[ti * 24 + 15]
                    qkv_ev_v[ti] = evac_dv[ti * 24 + 23]

            def phaseA_units(ti):
                units = [lambda: emit_tile_head(ti)]
                for b in range(2):
                    units.append(lambda b=b: emit_ln_stats(ti, b))
                    for q4 in range(8):
                        units.append(lambda b=b, q4=q4: emit_ln_tr(ti, b, q4))
                units.append(lambda: emit_xn_spill(ti))
                for ci in range(24):
                    units.append(lambda ci=ci: emit_qkv_chunk(ti, ci))
                return units

            def attn_tr(ti, gidx):
                par = gidx % 2
                qg3, k3 = QG32[ti % 2], K32[ti % 2]

                def _tp(t, gidx=gidx, ti=ti):
                    t.wait_ge(DV.h, qkv_ev_qk[ti])
                    emit_transpose(t, qg3[:, gidx * 64:(gidx + 1) * 64], pTR)
                    emit_transpose(t, k3[:, gidx * 64:(gidx + 1) * 64], pTR)
                block.tensor(_tp)
                nq = tr_state["idx"] - 2

                def _ev(v, nq=nq, par=par):
                    for off, dstt in ((0, QGT[par]), (1, KTT[par])):
                        n = nq + off
                        v.wait_ge(PE.h, tr_state["tr_pe"][n])
                        v.wait_ge(PE.h, A["scl_pe"][par])
                        bank = pTR[n % 2]
                        slot = (n // 2) % 4
                        view = bank[0:64, slot * 128:(slot + 1) * 128]
                        d_op(v, lambda view=view, dstt=dstt: v.tensor_copy(dstt[0:64, :], view))
                        tr_state["ev_dv"].append(DV.n)
                block.vector(_ev)
                A["evk_dv"][par] = DV.n

            def attn_sc(ti, gidx):
                par = gidx % 2

                def _sc(t, par=par, evk_dv=A["evk_dv"][par], exp_ac=A["exp_ac"][par],
                        norm_g_dv=A["norm_dv"][par]):
                    t.wait_ge(DV.h, evk_dv)
                    t.wait_ge(ACS.h, exp_ac)
                    t.wait_ge(DV.h, norm_g_dv)
                    inst = None
                    for j in range(GRP):
                        inst = t.matmul(pSC[par][:, j * 512:j * 512 + 128],
                                        KTT[par][32 * j:32 * j + 8, :],
                                        QGT[par][32 * j:32 * j + 8, :],
                                        start=True, stop=True, tile_position=(32 * j, 0))
                    p_mark(inst)
                    A["scl_pe"][par] = PE.n
                block.tensor(_sc)
                sc_pe = PE.n

                def _ex(s, sc_pe=sc_pe, par=par, avl=A["avl_pe"][par]):
                    s.wait_ge(PE.h, sc_pe)
                    s.wait_ge(PE.h, avl)
                    src = bass.AP(pSCh[par], 0, [[1024, 128], [512, 2], [1, 128]])
                    a_op(s, lambda: s.activation(ET[par][:], src, AF.Exp))
                block.scalar(_ex)
                A["exp_ac"][par] = ACS.n

            def attn_av(ti, gidx):
                par = gidx % 2
                tb = gidx * GRP
                va3 = VA[ti % 2]

                def _av(t, par=par, tb=tb, exp_ac=A["exp_ac"][par], ti=ti, va3=va3,
                        norm_g_dv=A["norm_dv"][par]):
                    t.wait_ge(ACS.h, exp_ac)
                    t.wait_ge(DV.h, qkv_ev_v[ti])
                    t.wait_ge(DV.h, norm_g_dv)
                    inst = None
                    for j in range(GRP):
                        inst = t.matmul(pSC[par][:, 128 + j * 9:128 + (j + 1) * 9],
                                        ET[par][:, j * 128:(j + 1) * 128],
                                        va3[:, (tb + j) * 9:(tb + j + 1) * 9],
                                        start=True, stop=True)
                    p_mark(inst)
                    A["avl_pe"][par] = PE.n
                block.tensor(_av)
                av_pe = PE.n

                def _nr(v, av_pe=av_pe, tb=tb, ti=ti, gidx=gidx, par=par):
                    v.wait_ge(PE.h, av_pe)
                    if gidx == 0:
                        v.wait_ge(S[f"GSP{ti % 2}"].h, o8_spill_gs[ti % 2])
                    rb = REC[par]
                    d_op(v, lambda: v.reciprocal(rb[:], bass.AP(pSCh[par], 136, [[1024, 128], [9, 2]])))
                    outv = bass.AP(_th(O8N[ti % 2]), tb, [[8 * TILE, 128], [TILE, 8], [1, 2]])
                    inv = bass.AP(pSCh[par], 128, [[1024, 128], [1, 8], [9, 2]])
                    recv = bass.AP(_th(rb), 0, [[2, 128], [0, 8], [1, 2]])
                    d_op(v, lambda: v.tensor_tensor(outv, inv, recv, ALU.mult), dep=DV.n)
                block.vector(_nr)
                A["norm_dv"][par] = DV.n

            def emit_attn_spill(ti):
                attn_norm_dv = max(A["norm_dv"])

                def _as(gq, ti=ti, attn_norm_dv=attn_norm_dv):
                    gq.wait_ge(DV.h, attn_norm_dv)
                    for j in range(4):
                        dst = bass.AP(_th(attn_sp), j * TOK + ti * TILE,
                                      [[32 * TOK, 128], [4 * TOK, 8], [1, TILE]])
                        dma(gq, S[f"GSP{ti % 2}"], dst, O8N[ti % 2][:])
                block.gpsimd(_as)
                o8_spill_gs[ti % 2] = S[f"GSP{ti % 2}"].n

            for u in phaseA_units(0):
                u()
            NSTEP = NGRP + 2
            for ti in range(NTILE):
                nxt = phaseA_units(ti + 1) if ti + 1 < NTILE else []
                k = 0
                for st in range(NSTEP):
                    if st < NGRP:
                        attn_tr(ti, st)
                    if 1 <= st <= NGRP:
                        attn_sc(ti, st - 1)
                    if st >= 2:
                        attn_av(ti, st - 2)
                    while k * NSTEP < (st + 1) * len(nxt):
                        nxt[k]()
                        k += 1
                assert k == len(nxt)
                attn_done_pe[ti % 2] = PE.n
                emit_attn_spill(ti)

        barrier()

        with contextlib.ExitStack() as scB:
            eB = scB.enter_context
            XN3 = eB(nc.sbuf_tensor("XN3", [128, ECH * TOK], BF16))
            WUPB = [eB(nc.sbuf_tensor(f"WUPB{i}", [128, 4096], BF16)) for i in range(2)]
            WGTB = [eB(nc.sbuf_tensor(f"WGTB{i}", [128, 4096], BF16)) for i in range(2)]
            SG = [eB(nc.sbuf_tensor(f"SG{i}", [128, 512], F32)) for i in range(2)]
            HCH = [eB(nc.sbuf_tensor(f"HCH{i}", [128, TOK], BF16)) for i in range(2)]
            WDS = [eB(nc.sbuf_tensor(f"WDS{i}", [128, 4096], BF16)) for i in range(2)]
            pUP = [eB(nc.psum_tensor(f"pUP{i}", [128, 512], F32)) for i in range(2)]
            pGA = [eB(nc.psum_tensor(f"pGA{i}", [128, 512], F32)) for i in range(2)]

            def _x3(sq):
                src = bass.AP(_th(xn_sp), 0, [[TOK, 128], [128 * TOK, ECH], [1, TOK]])
                dma(sq, S["SH3"], XN3[:], src)
            block.sync(_x3)
            xn3_ld = S["SH3"].n

            upg_ld = []
            mm_up_pe = {}
            mm_ga_pe = {}
            silu_ac = {}
            mult_dv = {}
            hsp_sx = [0, 0]
            wd_ld_gp = [0] * FCH
            wd_st_sw = [0] * FCH

            def emit_upg_load(fc):
                def _w(gq, fc=fc):
                    if fc >= 2:
                        gq.wait_ge(PE.h, mm_ga_pe[(fc - 2, TG3 - 1)])
                    for wsrc, wdst in ((wupT, WUPB[fc % 2]), (wgT, WGTB[fc % 2])):
                        src = bass.AP(_th(wsrc), fc * 128,
                                      [[DFF, 128], [128 * DFF, ECH], [1, 128]])
                        upg_ld.append(dma(gq, S[f"GUP{fc % 2}"], wdst[:], src))
                block.gpsimd(_w)

            emit_upg_load(0)
            emit_upg_load(1)

            for fc in range(FCH):
                ld_cnt = upg_ld[fc * 2 + 1]
                for tg in range(TG3):
                    def _mm(t, fc=fc, tg=tg, ld_cnt=ld_cnt):
                        t.wait_ge(S[f"GUP{fc % 2}"].h, ld_cnt)
                        if fc == 0 and tg == 0:
                            t.wait_ge(S["SH3"].h, xn3_ld)
                        g2 = fc * TG3 + tg - 2
                        key2 = (g2 // TG3, g2 % TG3) if g2 >= 0 else None
                        if key2 in mult_dv:
                            t.wait_ge(DV.h, mult_dv[key2])
                        if key2 in silu_ac:
                            t.wait_ge(ACS.h, silu_ac[key2])
                        inst = None
                        for ec in range(ECH):
                            rhs = XN3[:, ec * TOK + tg * 512: ec * TOK + tg * 512 + 512]
                            inst = t.matmul(pUP[tg % 2][:], WUPB[fc % 2][:, ec * 128:(ec + 1) * 128],
                                            rhs, start=(ec == 0), stop=(ec == ECH - 1))
                        mm_up_pe[(fc, tg)] = p_mark(inst)
                        inst = None
                        for ec in range(ECH):
                            rhs = XN3[:, ec * TOK + tg * 512: ec * TOK + tg * 512 + 512]
                            inst = t.matmul(pGA[tg % 2][:], WGTB[fc % 2][:, ec * 128:(ec + 1) * 128],
                                            rhs, start=(ec == 0), stop=(ec == ECH - 1))
                        mm_ga_pe[(fc, tg)] = p_mark(inst)
                    block.tensor(_mm)

                    def _si(s, fc=fc, tg=tg):
                        s.wait_ge(PE.h, mm_ga_pe[(fc, tg)])
                        g2 = fc * TG3 + tg - 2
                        key2 = (g2 // TG3, g2 % TG3) if g2 >= 0 else None
                        if key2 in mult_dv:
                            s.wait_ge(DV.h, mult_dv[key2])
                        silu_ac[(fc, tg)] = a_op(
                            s, lambda: s.activation(SG[tg % 2][:], pGA[tg % 2][:], AF.Silu))
                    block.scalar(_si)

                    def _ml(v, fc=fc, tg=tg):
                        v.wait_ge(PE.h, mm_up_pe[(fc, tg)])
                        v.wait_ge(ACS.h, silu_ac[(fc, tg)])
                        if tg == 0:
                            v.wait_ge(S[f"SH{fc % 2}"].h, hsp_sx[fc % 2])
                        dst = HCH[fc % 2][:, tg * 512:(tg + 1) * 512]
                        mult_dv[(fc, tg)] = d_op(
                            v, lambda: v.tensor_tensor(dst, pUP[tg % 2][:], SG[tg % 2][:], ALU.mult))
                    block.vector(_ml)

                def _hs(sq, fc=fc):
                    sq.wait_ge(DV.h, mult_dv[(fc, TG3 - 1)])
                    dst = bass.AP(_th(h_sp), fc * 128 * TOK, [[TOK, 128], [1, TOK]])
                    dma(sq, S[f"SH{fc % 2}"], dst, HCH[fc % 2][:])
                block.sync(_hs)
                hsp_sx[fc % 2] = S[f"SH{fc % 2}"].n

                if fc + 2 < FCH:
                    emit_upg_load(fc + 2)

                def _wdl(gq, fc=fc):
                    if fc >= 2:
                        gq.wait_ge(S[f"SWD{fc % 2}"].h, wd_st_sw[fc - 2])
                    src = bass.AP(_th(wdT), fc * 128 * EMB, [[EMB, 128], [1, EMB]])
                    wd_ld_gp[fc] = dma(gq, S[f"GWD{fc % 2}"], WDS[fc % 2][:], src)
                block.gpsimd(_wdl)

                def _wds(sq, fc=fc):
                    sq.wait_ge(S[f"GWD{fc % 2}"].h, wd_ld_gp[fc])
                    dst = bass.AP(_th(wd_bf), fc * 128 * EMB, [[EMB, 128], [1, EMB]])
                    wd_st_sw[fc] = dma(sq, S[f"SWD{fc % 2}"], dst, WDS[fc % 2][:])
                block.sync(_wds)

        barrier()

        with contextlib.ExitStack() as scC:
            eC = scC.enter_context
            H3B = eC(nc.sbuf_tensor("H3B", [128, FCH * 512], BF16))
            WDB = [eC(nc.sbuf_tensor(f"WDB{i}", [128, FCH * 128], BF16)) for i in range(2)]
            YB = [eC(nc.sbuf_tensor(f"YB{i}", [128, 512], F32)) for i in range(2)]
            pY = [eC(nc.psum_tensor(f"pY{i}", [128, 512], F32)) for i in range(2)]

            ymm_pe = {}
            yev_dv = {}
            ysp_sx = {}
            h3_ld = [0] * SUB
            wd_ld = {}

            def emit_wd_load(sub, dm):
                def _w(sq, sub=sub, dm=dm):
                    gi = sub * DMCH + dm
                    if gi >= 2:
                        pgi = gi - 2
                        sq.wait_ge(PE.h, ymm_pe[(pgi // DMCH, pgi % DMCH)])
                    src = bass.AP(_th(wd_bf), dm * 128,
                                  [[EMB, 128], [128 * EMB, FCH], [1, 128]])
                    wd_ld[(sub, dm)] = dma(sq, S[f"SWB{gi % 2}"], WDB[gi % 2][:], src)
                block.sync(_w)

            FH = 43
            for sub in range(SUB):
                def _h3(sq, sub=sub):
                    if sub >= 1:
                        sq.wait_ge(PE.h, ymm_pe[(sub - 1, DMCH - 1)])
                    srcA = bass.AP(_th(h_sp), sub * 512,
                                   [[TOK, 128], [128 * TOK, FH], [1, 512]])
                    ca = dma(sq, S["SH3"], H3B[:, 0:FH * 512], srcA)
                    srcB = bass.AP(_th(h_sp), FH * 128 * TOK + sub * 512,
                                   [[TOK, 128], [128 * TOK, FCH - FH], [1, 512]])
                    cb = dma(sq, S["SH3B"], H3B[:, FH * 512:], srcB)
                    h3_ld[sub] = (ca, cb)
                block.sync(_h3)
                if sub == 0:
                    emit_wd_load(0, 0)
                    emit_wd_load(0, 1)

                for dm in range(DMCH):
                    gi = sub * DMCH + dm

                    def _mm(t, sub=sub, dm=dm, gi=gi):
                        t.wait_ge(S[f"SWB{gi % 2}"].h, wd_ld[(sub, dm)])
                        t.wait_ge(S["SH3"].h, h3_ld[sub][0])
                        key2 = (sub, dm - 2) if dm >= 2 else (sub - 1, dm + DMCH - 2)
                        if key2 in yev_dv:
                            t.wait_ge(DV.h, yev_dv[key2])
                        inst = None
                        for fcc in range(FH):
                            inst = t.matmul(pY[gi % 2][:], WDB[gi % 2][:, fcc * 128:(fcc + 1) * 128],
                                            H3B[:, fcc * 512:(fcc + 1) * 512],
                                            start=(fcc == 0), stop=False)
                        t.wait_ge(S["SH3B"].h, h3_ld[sub][1])
                        for fcc in range(FH, FCH):
                            inst = t.matmul(pY[gi % 2][:], WDB[gi % 2][:, fcc * 128:(fcc + 1) * 128],
                                            H3B[:, fcc * 512:(fcc + 1) * 512],
                                            start=False, stop=(fcc == FCH - 1))
                        ymm_pe[(sub, dm)] = p_mark(inst)
                    block.tensor(_mm)
                    ngi = gi + 2
                    if ngi < SUB * DMCH:
                        emit_wd_load(ngi // DMCH, ngi % DMCH)

                    def _ev(v, sub=sub, dm=dm, gi=gi):
                        v.wait_ge(PE.h, ymm_pe[(sub, dm)])
                        key2 = (sub, dm - 2) if dm >= 2 else (sub - 1, dm + DMCH - 2)
                        if key2 in ysp_sx:
                            v.wait_ge(S[f"SY{(key2[0] * DMCH + key2[1]) % 2}"].h, ysp_sx[key2])
                        yev_dv[(sub, dm)] = d_op(
                            v, lambda: v.tensor_copy(YB[gi % 2][:], pY[gi % 2][:]))
                    block.vector(_ev)

                    def _ys(gq, sub=sub, dm=dm, gi=gi):
                        gq.wait_ge(DV.h, yev_dv[(sub, dm)])
                        dst = bass.AP(_th(y_sp), dm * 128 * TOK + sub * 512,
                                      [[TOK, 128], [1, 512]])
                        ysp_sx[(sub, dm)] = dma(gq, S[f"SY{gi % 2}"], dst, YB[gi % 2][:])
                    block.gpsimd(_ys)

        barrier()

        with contextlib.ExitStack() as scD:
            eD = scD.enter_context
            YL = [eD(nc.sbuf_tensor(f"YL{i}", [128, EMB], F32)) for i in range(2)]
            AL = [eD(nc.sbuf_tensor(f"AL{i}", [128, EMB], F32)) for i in range(2)]
            XL = [eD(nc.sbuf_tensor(f"XL{i}", [128, EMB], F32)) for i in range(2)]
            SFM = eD(nc.sbuf_tensor("SFM", [128, EMB], F32))
            OT = [eD(nc.sbuf_tensor(f"OT{i}", [128, EMB], F32)) for i in range(2)]
            pTC = [eD(nc.psum_tensor(f"pTC{i}", [128, 512], F32)) for i in range(2)]

            add_dv = [0] * (TOK // 128)
            trc_pe = {}
            oadd_dv = {}
            out_g = [0, 0]
            sfm_tr_pe = 0

            for blk in range(TOK // 128):
                def _ld(sq, blk=blk):
                    if blk >= 2:
                        sq.wait_ge(DV.h, add_dv[blk - 2])
                        sq.wait_ge(S[f"GOUT{blk % 2}"].h, out_g[blk % 2])
                    ysrc = bass.AP(_th(y_sp), blk * 128,
                                   [[TOK, 128], [128 * TOK, ECH], [1, 128]])
                    dma(sq, S[f"SCL{blk % 2}"], YL[blk % 2][:], ysrc)
                    asrc = bass.AP(_th(attn_sp), blk * 128,
                                   [[TOK, 128], [128 * TOK, ECH], [1, 128]])
                    dma(sq, S[f"SCL{blk % 2}"], AL[blk % 2][:], asrc)
                    xsrc = bass.AP(_th(x_in), blk * 128 * EMB, [[EMB, 128], [1, EMB]])
                    dma(sq, S[f"SCL{blk % 2}"], XL[blk % 2][:], xsrc)
                block.sync(_ld)
                here = S[f"SCL{blk % 2}"].n

                def _ad(v, blk=blk, here=here, sfm_tr_pe=sfm_tr_pe):
                    v.wait_ge(S[f"SCL{blk % 2}"].h, here)
                    if blk >= 1:
                        v.wait_ge(PE.h, sfm_tr_pe)
                    add_dv[blk] = d_op(
                        v, lambda: v.tensor_tensor(SFM[:], YL[blk % 2][:], AL[blk % 2][:], ALU.add))
                block.vector(_ad)

                for q2 in range(16):
                    def _tp(t, blk=blk, q2=q2):
                        t.wait_ge(DV.h, add_dv[blk])
                        for k in range(2):
                            c = q2 * 2 + k
                            n = blk * 32 + c
                            bank = pTC[n % 2]
                            slot = (n // 2) % 4
                            if n >= 2:
                                t.wait_ge(DV.h, oadd_dv.get(n - 2, 0))
                            inst = t.transpose(bank[:, slot * 128:(slot + 1) * 128],
                                               SFM[:, c * 128:(c + 1) * 128], idf[:])
                            trc_pe[n] = p_mark(inst)
                    block.tensor(_tp)

                    def _oa(v, blk=blk, q2=q2):
                        for k in range(2):
                            c = q2 * 2 + k
                            n = blk * 32 + c
                            if c == 0:
                                v.wait_ge(S[f"GOUT{blk % 2}"].h, out_g[blk % 2])
                            v.wait_ge(PE.h, trc_pe[n])
                            bank = pTC[n % 2]
                            slot = (n // 2) % 4
                            view = bank[:, slot * 128:(slot + 1) * 128]
                            dst = OT[blk % 2][:, c * 128:(c + 1) * 128]
                            xsl = XL[blk % 2][:, c * 128:(c + 1) * 128]
                            oadd_dv[n] = d_op(
                                v, lambda view=view, dst=dst, xsl=xsl:
                                v.tensor_tensor(dst, view, xsl, ALU.add))
                    block.vector(_oa)
                sfm_tr_pe = PE.n

                def _ow(gq, blk=blk):
                    gq.wait_ge(DV.h, oadd_dv[blk * 32 + 31])
                    dst = bass.AP(_th(out_h), blk * 128 * EMB, [[EMB, 128], [1, EMB]])
                    dma(gq, S[f"GOUT{blk % 2}"], dst, OT[blk % 2][:])
                block.gpsimd(_ow)
                out_g[blk % 2] = S[f"GOUT{blk % 2}"].n

            def _fin(gq):
                gq.wait_ge(S["GOUT0"].h, S["GOUT0"].n)
                gq.wait_ge(S["GOUT1"].h, S["GOUT1"].n)
            block.gpsimd(_fin)

    return nc


def kernel(L0_attn_x, L0_attn_Wq, L0_attn_bq, L0_attn_Wk, L0_attn_bk,
           L0_attn_Wv, L0_attn_bv, L0_ffn_W_gate, L0_ffn_W_up, L0_ffn_W_down):
    x = np.asarray(L0_attn_x, dtype=np.float32).reshape(16384, 4096)
    wq = np.asarray(L0_attn_Wq, dtype=np.float32)
    wk = np.asarray(L0_attn_Wk, dtype=np.float32)
    wv = np.asarray(L0_attn_Wv, dtype=np.float32)
    shared = {
        "wq_r": np.ascontiguousarray(np.transpose(wq, (0, 2, 1)).reshape(4096, 4096)),
        "wk_r": np.ascontiguousarray(np.transpose(wk, (0, 2, 1)).reshape(4096, 1024)),
        "wv_r": np.ascontiguousarray(np.transpose(wv, (0, 2, 1)).reshape(4096, 1024)),
        "bq": np.ascontiguousarray(np.asarray(L0_attn_bq, dtype=np.float32)),
        "bk": np.ascontiguousarray(np.asarray(L0_attn_bk, dtype=np.float32)),
        "bv": np.ascontiguousarray(np.asarray(L0_attn_bv, dtype=np.float32)),
        "wupT": np.ascontiguousarray(np.asarray(L0_ffn_W_up, dtype=np.float32).T),
        "wgT": np.ascontiguousarray(np.asarray(L0_ffn_W_gate, dtype=np.float32).T),
        "wdT": np.ascontiguousarray(np.asarray(L0_ffn_W_down, dtype=np.float32).T),
        "idn": np.eye(128, dtype=np.float32),
    }
    nc = build_nc()
    in_maps = []
    for c in range(8):
        m = dict(shared)
        m["x"] = np.ascontiguousarray(x[c * TOK:(c + 1) * TOK])
        in_maps.append(m)
    res = run_bass_kernel_spmd(nc, in_maps, core_ids=list(range(8)))
    out = np.concatenate([r["out"] for r in res.results], axis=0)
    return out.reshape(1, 16384, 4096)


# version 09 (final)
# speedup vs baseline: 1.1314x; 1.0430x over previous
import numpy as np
import contextlib
import concourse.bass as bass
import concourse.mybir as mybir
from concourse.bass_utils import run_bass_kernel_spmd

F32 = mybir.dt.float32
BF16 = mybir.dt.bfloat16
AF = mybir.ActivationFunctionType
ALU = mybir.AluOpType
AX = mybir.AxisListType

TOK = 2048
EMB = 4096
ECH = 32
DFF = 11008
FCH = 86
TILE = 256
NTILE = TOK // TILE
BLK = 128
NBLK = TOK // BLK
GRP = 4
NGRP = TILE // GRP
TG3 = 4
SUB = 4
DMCH = 32
INV_SQRT_D = 0.08838834764831845
INV_EMB = 1.0 / 4096.0
LN_EPS = 1e-5


def _th(x):
    return x.tensor if hasattr(x, "tensor") else x


class _Ctr:
    __slots__ = ("h", "n")

    def __init__(self, h):
        self.h = h
        self.n = 0


def build_nc():
    nc = bass.Bass("TRN2", target_bir_lowering=False)

    x_in = nc.dram_tensor("x", [TOK, EMB], F32, kind="ExternalInput")
    wq_r = nc.dram_tensor("wq_r", [EMB, 4096], F32, kind="ExternalInput")
    wk_r = nc.dram_tensor("wk_r", [EMB, 1024], F32, kind="ExternalInput")
    wv_r = nc.dram_tensor("wv_r", [EMB, 1024], F32, kind="ExternalInput")
    bq_in = nc.dram_tensor("bq", [128, 32], F32, kind="ExternalInput")
    bk_in = nc.dram_tensor("bk", [128, 8], F32, kind="ExternalInput")
    bv_in = nc.dram_tensor("bv", [128, 8], F32, kind="ExternalInput")
    wupT = nc.dram_tensor("wupT", [EMB, DFF], F32, kind="ExternalInput")
    wgT = nc.dram_tensor("wgT", [EMB, DFF], F32, kind="ExternalInput")
    wdT = nc.dram_tensor("wdT", [DFF, EMB], F32, kind="ExternalInput")
    idn = nc.dram_tensor("idn", [128, 128], F32, kind="ExternalInput")
    out_h = nc.dram_tensor("out", [TOK, EMB], F32, kind="ExternalOutput")

    wqg_bf = nc.dram_tensor("wqg_bf", [EMB, 1024], BF16, kind="Internal")
    wd_bf = nc.dram_tensor("wd_bf", [DFF, EMB], BF16, kind="Internal")
    xn_sp = nc.dram_tensor("xn_sp", [EMB, TOK], BF16, kind="Internal")
    attn_sp = nc.dram_tensor("attn_sp", [EMB, TOK], F32, kind="Internal")
    h_sp = nc.dram_tensor("h_sp", [DFF, TOK], BF16, kind="Internal")
    y_sp = nc.dram_tensor("y_sp", [EMB, TOK], F32, kind="Internal")

    g = contextlib.ExitStack()
    with g:
        es = g.enter_context
        idb = es(nc.sbuf_tensor("idb", [128, 128], BF16))
        idf = es(nc.sbuf_tensor("idf", [128, 128], F32))
        bqs = es(nc.sbuf_tensor("bqs", [128, 32], F32))
        bks = es(nc.sbuf_tensor("bks", [128, 8], F32))
        bvs = es(nc.sbuf_tensor("bvs", [128, 8], F32))
        bqg = es(nc.sbuf_tensor("bqg", [128, 8], F32))
        bt1 = es(nc.sbuf_tensor("bt1", [128, 8], F32))
        bt2 = es(nc.sbuf_tensor("bt2", [128, 8], F32))

        names = (["DV", "AC", "PE", "GM"]
                 + ["SX0", "SX1", "SW0", "SW1", "SW2", "SXN0", "SXN1",
                    "GKV0", "GKV1", "GKV2", "GSP0", "GSP1"]
                 + ["SQS0", "SQS1", "SQW0", "SQW1", "SIN", "GIN"]
                 + ["GUP0", "GUP1", "SH0", "SH1", "GWD0", "GWD1", "SWD0", "SWD1"]
                 + ["SH3", "SH3B", "SWB0", "SWB1", "SWB2", "SY0", "SY1"]
                 + ["SCL0", "SCL1", "GCL0", "GCL1", "GOUT0", "GOUT1"])
        S = {n: _Ctr(es(nc.semaphore(n))) for n in names}
        DV, ACS, PE, GM = S["DV"], S["AC"], S["PE"], S["GM"]

        block = es(nc.Block())

        def d_op(v, fn, dep=None):
            if dep is not None:
                v.wait_ge(DV.h, dep)
            fn().then_inc(DV.h, 1)
            DV.n += 1
            return DV.n

        def a_op(s, fn):
            s.wait_ge(ACS.h, ACS.n)
            fn().then_inc(ACS.h, 1)
            ACS.n += 1
            return ACS.n

        def p_mark(inst):
            inst.then_inc(PE.h, 1)
            PE.n += 1
            return PE.n

        def dma(q, sem, out, in_):
            q.dma_start(out=out, in_=in_).then_inc(sem.h, 16)
            sem.n += 16
            return sem.n

        def barrier():
            cnts = [(c.h, c.n) for c in S.values()]
            for engname in ("tensor", "vector", "scalar", "sync", "gpsimd"):
                def _mk(e):
                    for h, n in cnts:
                        e.wait_ge(h, n)
                getattr(block, engname)(_mk)

        tr_state = {"idx": 0, "ev_dv": [], "tr_pe": []}

        def emit_transpose(t, src, psums):
            n = tr_state["idx"]
            bank = psums[n % 2]
            slot = (n // 2) % 4
            view = bank[:, slot * 128:(slot + 1) * 128]
            if n >= 2:
                t.wait_ge(DV.h, tr_state["ev_dv"][n - 2])
            inst = t.transpose(view, src, idb[:])
            p_mark(inst)
            tr_state["tr_pe"].append(PE.n)
            tr_state["idx"] += 1
            return view, n

        with contextlib.ExitStack() as sc0:
            e0 = sc0.enter_context
            WQS = [e0(nc.sbuf_tensor(f"WQS{i}", [128, 4096], F32)) for i in range(2)]
            s01 = e0(nc.sbuf_tensor("s01", [128, 1024], F32))
            s23 = e0(nc.sbuf_tensor("s23", [128, 1024], F32))
            ssum = e0(nc.sbuf_tensor("ssum", [128, 1024], F32))
            wqg_t = [e0(nc.sbuf_tensor(f"wqgt{i}", [128, 1024], BF16)) for i in range(2)]

            def _gp(gq):
                dma(gq, S["GIN"], idb[:], idn.ap())
            block.gpsimd(_gp)

            def _sy(sq):
                dma(sq, S["SIN"], idf[:], idn.ap())
                dma(sq, S["SIN"], bqs[:], bq_in.ap())
                dma(sq, S["SIN"], bks[:], bk_in.ap())
                dma(sq, S["SIN"], bvs[:], bv_in.ap())
            block.sync(_sy)

            def _dv(v):
                v.wait_ge(S["SIN"].h, S["SIN"].n)
                v.wait_ge(S["GIN"].h, S["GIN"].n)
                bqh = _th(bqs)
                views = [bass.AP(bqh, j, [[32, 128], [4, 8]]) for j in range(4)]
                d_op(v, lambda: v.tensor_tensor(bt1[:], views[0], views[1], ALU.add))
                d_op(v, lambda: v.tensor_tensor(bt2[:], views[2], views[3], ALU.add))
                d_op(v, lambda: v.tensor_tensor(bt1[:], bt1[:], bt2[:], ALU.add), dep=DV.n)
                d_op(v, lambda: v.tensor_scalar(bqg[:], bt1[:], INV_SQRT_D, None, ALU.mult), dep=DV.n)
            block.vector(_dv)

            s23_dv = [0] * ECH
            st_sq = [0] * ECH
            ld_sq = [0] * ECH
            for ec in range(ECH):
                buf = ec % 2

                def _ld(sq, ec=ec, buf=buf):
                    if ec >= 2:
                        sq.wait_ge(DV.h, s23_dv[ec - 2])
                    src = bass.AP(_th(wq_r), ec * 128 * 4096, [[4096, 128], [1, 4096]])
                    ld_sq[ec] = dma(sq, S[f"SQS{buf}"], WQS[buf][:], src)
                block.sync(_ld)

                def _gs(v, ec=ec, buf=buf):
                    v.wait_ge(S[f"SQS{buf}"].h, ld_sq[ec])
                    if ec >= 2:
                        v.wait_ge(S[f"SQW{buf}"].h, st_sq[ec - 2])
                    wh = _th(WQS[buf])
                    vw = [bass.AP(wh, j * 128, [[4096, 128], [512, 8], [1, 128]]) for j in range(4)]
                    d_op(v, lambda: v.tensor_tensor(s01[:], vw[0], vw[1], ALU.add))
                    d_op(v, lambda: v.tensor_tensor(s23[:], vw[2], vw[3], ALU.add))
                    s23_dv[ec] = DV.n
                    d_op(v, lambda: v.tensor_tensor(ssum[:], s01[:], s23[:], ALU.add), dep=DV.n)
                    d_op(v, lambda: v.tensor_scalar(wqg_t[buf][:], ssum[:], INV_SQRT_D, None, ALU.mult), dep=DV.n)
                block.vector(_gs)

                def _st(sq, ec=ec, buf=buf):
                    sq.wait_ge(DV.h, DV.n)
                    dst = bass.AP(_th(wqg_bf), ec * 128 * 1024, [[1024, 128], [1, 1024]])
                    st_sq[ec] = dma(sq, S[f"SQW{buf}"], dst, wqg_t[buf][:])
                block.sync(_st)

        barrier()

        with contextlib.ExitStack() as scA:
            eA = scA.enter_context
            XB = [eA(nc.sbuf_tensor(f"XB{i}", [128, EMB], F32)) for i in range(2)]
            XSQ = eA(nc.sbuf_tensor("XSQ", [128, EMB], F32))
            XNT = [eA(nc.sbuf_tensor(f"XNT{i}", [128, EMB], BF16)) for i in range(2)]
            XNF = eA(nc.sbuf_tensor("XNF", [128, ECH * TILE], BF16))
            WB = [eA(nc.sbuf_tensor(f"WB{i}", [128, 4096], BF16)) for i in range(3)]
            QG32 = [eA(nc.sbuf_tensor(f"QG32{i}", [128, TILE * 32], BF16)) for i in range(2)]
            K32 = [eA(nc.sbuf_tensor(f"K32{i}", [128, TILE * 32], BF16)) for i in range(2)]
            VA = [eA(nc.sbuf_tensor(f"VA{i}", [128, TILE * 9], BF16)) for i in range(2)]
            QGT = [eA(nc.sbuf_tensor(f"QGT{i}", [128, 128], BF16)) for i in range(2)]
            KTT = [eA(nc.sbuf_tensor(f"KTT{i}", [128, 128], BF16)) for i in range(2)]
            ET = [eA(nc.sbuf_tensor(f"ET{i}", [128, 512], BF16)) for i in range(2)]
            O8N = [eA(nc.sbuf_tensor(f"O8N{i}", [128, 8 * TILE], F32)) for i in range(2)]
            REC = [eA(nc.sbuf_tensor(f"REC{i}", [128, 4], F32)) for i in range(2)]
            st1 = eA(nc.sbuf_tensor("st1", [128, 1], F32))
            st2 = eA(nc.sbuf_tensor("st2", [128, 1], F32))
            stm = eA(nc.sbuf_tensor("stm", [128, 1], F32))
            stv = eA(nc.sbuf_tensor("stv", [128, 1], F32))
            str_ = eA(nc.sbuf_tensor("str", [128, 1], F32))
            stl = eA(nc.sbuf_tensor("stl", [128, 1], F32))
            stn = eA(nc.sbuf_tensor("stn", [128, 1], F32))
            pQKV = [eA(nc.psum_tensor(f"pQKV{i}", [128, 512], F32)) for i in range(2)]
            pTR = [eA(nc.psum_tensor(f"pTR{i}", [128, 512], BF16)) for i in range(2)]
            pSC = eA(nc.psum_tensor("pSC", [128, 2048], F32))
            pSCh = _th(pSC)

            def _ini(gq):
                for i in range(2):
                    gq.memset(QG32[i][:], 0.0).then_inc(GM.h, 1); GM.n += 1
                    gq.memset(K32[i][:], 0.0).then_inc(GM.h, 1); GM.n += 1
                    gq.memset(bass.AP(_th(VA[i]), 8, [[TILE * 9, 128], [9, TILE]]),
                              1.0).then_inc(GM.h, 1)
                    GM.n += 1
            block.gpsimd(_ini)
            gm_ready = GM.n

            norm_dv = [0, 0]
            x_ld = [0] * (2 * NTILE + 2)
            trlast_pe = [0, 0]
            xn_spill_sx = [0] * (NTILE + 1)
            qkv_mm_last = [0] * (NTILE + 1)
            qkv_ev_qk = [0] * (NTILE + 1)
            qkv_ev_v = [0] * (NTILE + 1)
            attn_done_pe = [0, 0]
            o8_spill_gs = [0, 0]
            wld = []
            wmm_pe = []
            evac_dv = []
            A = {"exp_ac": 0, "norm_dv": 0, "scl_pe": [0, 0], "avl_pe": [0, 0]}

            def emit_w_load(ti, ci):
                gi = ti * 24 + ci
                m = ci % 8
                if ci < 8:
                    eng = "sync"
                    sem = S[f"SW{gi % 3}"]
                    src = bass.AP(_th(wqg_bf), m * 128,
                                  [[1024, 128], [128 * 1024, 32], [1, 128]])
                else:
                    eng = "gpsimd"
                    sem = S[f"GKV{gi % 3}"]
                    src = bass.AP(_th(wk_r if ci < 16 else wv_r), m * 128,
                                  [[1024, 128], [128 * 1024, 32], [1, 128]])

                def _w(q, gi=gi, sem=sem, src=src):
                    if gi >= 3:
                        q.wait_ge(PE.h, wmm_pe[gi - 3])
                    wld.append((sem, dma(q, sem, WB[gi % 3][:], src)))
                getattr(block, eng)(_w)

            def emit_tile_head(ti):
                def _xl(sq, ti=ti):
                    for b in range(2):
                        gb = ti * 2 + b
                        if gb >= 2:
                            sq.wait_ge(DV.h, norm_dv[gb % 2])
                        src = bass.AP(_th(x_in), gb * 128 * EMB, [[EMB, 128], [1, EMB]])
                        x_ld[gb] = dma(sq, S[f"SX{gb % 2}"], XB[gb % 2][:], src)
                block.sync(_xl)
                if ti == 0:
                    for ci in range(3):
                        emit_w_load(0, ci)

            def emit_ln_stats(ti, b):
                gb = ti * 2 + b

                def _st(v, gb=gb):
                    v.wait_ge(S[f"SX{gb % 2}"].h, x_ld[gb])
                    xb = XB[gb % 2]
                    d_op(v, lambda: v.reduce_sum(st1[:], xb[:], AX.X))
                    d_op(v, lambda: v.tensor_tensor(XSQ[:], xb[:], xb[:], ALU.mult))
                    d_op(v, lambda: v.reduce_sum(st2[:], XSQ[:], AX.X), dep=DV.n)
                    d_op(v, lambda: v.tensor_scalar(stm[:], st1[:], INV_EMB, None, ALU.mult), dep=DV.n)
                    d_op(v, lambda: v.tensor_scalar(st2[:], st2[:], INV_EMB, None, ALU.mult), dep=DV.n)
                    d_op(v, lambda: v.tensor_tensor(st1[:], stm[:], stm[:], ALU.mult), dep=DV.n)
                    d_op(v, lambda: v.tensor_tensor(stv[:], st2[:], st1[:], ALU.subtract), dep=DV.n)
                    d_op(v, lambda: v.tensor_scalar(stv[:], stv[:], LN_EPS, None, ALU.add), dep=DV.n)
                block.vector(_st)
                var_dv = DV.n

                def _l1(s, var_dv=var_dv):
                    s.wait_ge(DV.h, var_dv)
                    a_op(s, lambda: s.activation(stl[:], stv[:], AF.Ln))
                block.scalar(_l1)
                ln_ac = ACS.n

                def _l2(v, ln_ac=ln_ac):
                    v.wait_ge(ACS.h, ln_ac)
                    d_op(v, lambda: v.tensor_scalar(stl[:], stl[:], -0.5, None, ALU.mult))
                block.vector(_l2)
                lh_dv = DV.n

                def _l3(s, lh_dv=lh_dv):
                    s.wait_ge(DV.h, lh_dv)
                    a_op(s, lambda: s.activation(str_[:], stl[:], AF.Exp))
                block.scalar(_l3)
                rstd_ac = ACS.n

                def _nm(v, gb=gb, rstd_ac=rstd_ac):
                    v.wait_ge(ACS.h, rstd_ac)
                    if gb >= 2:
                        v.wait_ge(PE.h, trlast_pe[gb % 2])
                    d_op(v, lambda: v.tensor_scalar(stn[:], stm[:], str_[:], -1.0, ALU.mult, ALU.mult))
                    d_op(v, lambda: v.tensor_scalar(XNT[gb % 2][:], XB[gb % 2][:], str_[:], stn[:],
                                                    ALU.mult, ALU.add), dep=DV.n)
                    norm_dv[gb % 2] = DV.n
                block.vector(_nm)

            def emit_ln_tr(ti, b, q4):
                gb = ti * 2 + b
                nrm_here = norm_dv[gb % 2]
                for half in range(2):
                    views = []

                    def _tp(t, q4=q4, half=half, gb=gb, b=b, nrm_here=nrm_here, ti=ti):
                        t.wait_ge(DV.h, nrm_here)
                        if b == 0 and q4 == 0 and half == 0:
                            if ti > 0:
                                t.wait_ge(S[f"SXN{(ti - 1) % 2}"].h, xn_spill_sx[ti - 1])
                            t.wait_ge(PE.h, qkv_mm_last[ti - 1] if ti > 0 else 0)
                        for k in range(2):
                            c = q4 * 4 + half * 2 + k
                            view, n = emit_transpose(t, XNT[gb % 2][:, c * 128:(c + 1) * 128], pTR)
                            views.append((view, n, c))
                    block.tensor(_tp)

                    def _ev(v, views=views, b=b):
                        for view, n, c in views:
                            v.wait_ge(PE.h, tr_state["tr_pe"][n])
                            dst = XNF[:, c * TILE + b * 128: c * TILE + b * 128 + 128]
                            d_op(v, lambda view=view, dst=dst: v.tensor_copy(dst, view))
                            tr_state["ev_dv"].append(DV.n)
                    block.vector(_ev)
                if b == 1 and q4 == 7:
                    trlast_pe[0] = PE.n
                    trlast_pe[1] = PE.n

            def emit_xn_spill(ti):
                def _xs(sq, ti=ti):
                    sq.wait_ge(DV.h, DV.n)
                    dst = bass.AP(_th(xn_sp), ti * TILE,
                                  [[TOK, 128], [128 * TOK, ECH], [1, TILE]])
                    dma(sq, S[f"SXN{ti % 2}"], dst, XNF[:])
                block.sync(_xs)
                xn_spill_sx[ti] = S[f"SXN{ti % 2}"].n

            def emit_qkv_chunk(ti, ci):
                gi = ti * 24 + ci
                ln_evacs_dv = DV.n

                def _mm(t, ti=ti, ci=ci, gi=gi, ln_evacs_dv=ln_evacs_dv):
                    sem, cnt = wld[gi]
                    t.wait_ge(sem.h, cnt)
                    t.wait_ge(DV.h, ln_evacs_dv)
                    if gi >= 2:
                        t.wait_ge(DV.h, evac_dv[gi - 2])
                    ps = pQKV[gi % 2]
                    inst = None
                    for ec in range(ECH):
                        inst = t.matmul(ps[:, 0:TILE], WB[gi % 3][:, ec * 128:(ec + 1) * 128],
                                        XNF[:, ec * TILE:(ec + 1) * TILE],
                                        start=(ec == 0), stop=(ec == ECH - 1))
                    wmm_pe.append(p_mark(inst))
                block.tensor(_mm)
                if gi + 3 < NTILE * 24:
                    nxt = gi + 3
                    emit_w_load(nxt // 24, nxt % 24)

                def _evc(v, ti=ti, ci=ci, gi=gi):
                    v.wait_ge(PE.h, wmm_pe[gi])
                    if ci == 0:
                        v.wait_ge(PE.h, attn_done_pe[ti % 2])
                    m = ci % 8
                    ps = pQKV[gi % 2]
                    if ci < 8:
                        dst = bass.AP(_th(QG32[ti % 2]), m, [[TILE * 32, 128], [32, TILE]])
                        bias = bqg[:, m:m + 1]
                    elif ci < 16:
                        dst = bass.AP(_th(K32[ti % 2]), m, [[TILE * 32, 128], [32, TILE]])
                        bias = bks[:, m:m + 1]
                    else:
                        dst = bass.AP(_th(VA[ti % 2]), m, [[TILE * 9, 128], [9, TILE]])
                        bias = bvs[:, m:m + 1]
                    if gi == 0:
                        v.wait_ge(GM.h, gm_ready)
                    d_op(v, lambda: v.tensor_scalar(dst, ps[:, 0:TILE], bias, None, ALU.add))
                    evac_dv.append(DV.n)
                block.vector(_evc)
                if ci == 23:
                    qkv_mm_last[ti] = PE.n
                    qkv_ev_qk[ti] = evac_dv[ti * 24 + 15]
                    qkv_ev_v[ti] = evac_dv[ti * 24 + 23]

            def phaseA_units(ti):
                units = [lambda: emit_tile_head(ti)]
                for b in range(2):
                    units.append(lambda b=b: emit_ln_stats(ti, b))
                    for q4 in range(8):
                        units.append(lambda b=b, q4=q4: emit_ln_tr(ti, b, q4))
                units.append(lambda: emit_xn_spill(ti))
                for ci in range(24):
                    units.append(lambda ci=ci: emit_qkv_chunk(ti, ci))
                return units

            def emit_attn_group(ti, gidx):
                tb = gidx * GRP
                par = gidx % 2
                qg3, k3, va3 = QG32[ti % 2], K32[ti % 2], VA[ti % 2]

                def _tp(t, gidx=gidx, ti=ti):
                    t.wait_ge(DV.h, qkv_ev_qk[ti])
                    emit_transpose(t, qg3[:, gidx * 128:(gidx + 1) * 128], pTR)
                    emit_transpose(t, k3[:, gidx * 128:(gidx + 1) * 128], pTR)
                block.tensor(_tp)
                nq = tr_state["idx"] - 2

                def _ev(v, nq=nq, par=par):
                    for off, dstt in ((0, QGT[par]), (1, KTT[par])):
                        n = nq + off
                        v.wait_ge(PE.h, tr_state["tr_pe"][n])
                        v.wait_ge(PE.h, A["scl_pe"][par])
                        bank = pTR[n % 2]
                        slot = (n // 2) % 4
                        view = bank[:, slot * 128:(slot + 1) * 128]
                        d_op(v, lambda view=view, dstt=dstt: v.tensor_copy(dstt[:], view))
                        tr_state["ev_dv"].append(DV.n)
                block.vector(_ev)
                evk_dv = DV.n

                def _sc(t, par=par, evk_dv=evk_dv, exp_ac=A["exp_ac"], norm_g_dv=A["norm_dv"]):
                    t.wait_ge(DV.h, evk_dv)
                    t.wait_ge(ACS.h, exp_ac)
                    t.wait_ge(DV.h, norm_g_dv)
                    inst = None
                    for j in range(GRP):
                        inst = t.matmul(pSC[:, j * 512:j * 512 + 128],
                                        KTT[par][32 * j:32 * j + 8, :],
                                        QGT[par][32 * j:32 * j + 8, :],
                                        start=True, stop=True, tile_position=(32 * j, 0))
                    p_mark(inst)
                    A["scl_pe"][par] = PE.n
                block.tensor(_sc)
                sc_pe = PE.n

                def _ex(s, sc_pe=sc_pe, par=par):
                    s.wait_ge(PE.h, sc_pe)
                    s.wait_ge(PE.h, A["avl_pe"][par])
                    src = bass.AP(pSCh, 0, [[2048, 128], [512, 4], [1, 128]])
                    a_op(s, lambda: s.activation(ET[par][:], src, AF.Exp))
                block.scalar(_ex)
                A["exp_ac"] = ACS.n

                def _av(t, par=par, tb=tb, exp_ac=A["exp_ac"], norm_g_dv=A["norm_dv"], ti=ti,
                        va3=va3):
                    t.wait_ge(ACS.h, exp_ac)
                    t.wait_ge(DV.h, qkv_ev_v[ti])
                    t.wait_ge(DV.h, norm_g_dv)
                    inst = None
                    for j in range(GRP):
                        inst = t.matmul(pSC[:, 128 + j * 9:128 + (j + 1) * 9],
                                        ET[par][:, j * 128:(j + 1) * 128],
                                        va3[:, (tb + j) * 9:(tb + j + 1) * 9],
                                        start=True, stop=True)
                    p_mark(inst)
                    A["avl_pe"][par] = PE.n
                block.tensor(_av)
                av_pe = PE.n

                def _nr(v, av_pe=av_pe, tb=tb, ti=ti, gidx=gidx, par=par):
                    v.wait_ge(PE.h, av_pe)
                    if gidx == 0:
                        v.wait_ge(S[f"GSP{ti % 2}"].h, o8_spill_gs[ti % 2])
                    rb = REC[par]
                    d_op(v, lambda: v.reciprocal(rb[:], bass.AP(pSCh, 136, [[2048, 128], [9, 4]])))
                    outv = bass.AP(_th(O8N[ti % 2]), tb, [[8 * TILE, 128], [TILE, 8], [1, 4]])
                    inv = bass.AP(pSCh, 128, [[2048, 128], [1, 8], [9, 4]])
                    recv = bass.AP(_th(rb), 0, [[4, 128], [0, 8], [1, 4]])
                    d_op(v, lambda: v.tensor_tensor(outv, inv, recv, ALU.mult), dep=DV.n)
                block.vector(_nr)
                A["norm_dv"] = DV.n

            def emit_attn_spill(ti):
                attn_norm_dv = A["norm_dv"]

                def _as(gq, ti=ti, attn_norm_dv=attn_norm_dv):
                    gq.wait_ge(DV.h, attn_norm_dv)
                    for j in range(4):
                        dst = bass.AP(_th(attn_sp), j * TOK + ti * TILE,
                                      [[32 * TOK, 128], [4 * TOK, 8], [1, TILE]])
                        dma(gq, S[f"GSP{ti % 2}"], dst, O8N[ti % 2][:])
                block.gpsimd(_as)
                o8_spill_gs[ti % 2] = S[f"GSP{ti % 2}"].n

            for u in phaseA_units(0):
                u()
            for ti in range(NTILE):
                nxt = phaseA_units(ti + 1) if ti + 1 < NTILE else []
                k = 0
                for gidx in range(NGRP):
                    emit_attn_group(ti, gidx)
                    while k * NGRP < (gidx + 1) * len(nxt):
                        nxt[k]()
                        k += 1
                assert k == len(nxt)
                attn_done_pe[ti % 2] = PE.n
                emit_attn_spill(ti)

        barrier()

        with contextlib.ExitStack() as scB:
            eB = scB.enter_context
            XN3 = eB(nc.sbuf_tensor("XN3", [128, ECH * TOK], BF16))
            WUPB = [eB(nc.sbuf_tensor(f"WUPB{i}", [128, 4096], BF16)) for i in range(2)]
            WGTB = [eB(nc.sbuf_tensor(f"WGTB{i}", [128, 4096], BF16)) for i in range(2)]
            SG = [eB(nc.sbuf_tensor(f"SG{i}", [128, 512], F32)) for i in range(2)]
            HCH = [eB(nc.sbuf_tensor(f"HCH{i}", [128, TOK], BF16)) for i in range(2)]
            WDS = [eB(nc.sbuf_tensor(f"WDS{i}", [128, 4096], BF16)) for i in range(2)]
            pUP = [eB(nc.psum_tensor(f"pUP{i}", [128, 512], F32)) for i in range(2)]
            pGA = [eB(nc.psum_tensor(f"pGA{i}", [128, 512], F32)) for i in range(2)]

            def _x3(sq):
                src = bass.AP(_th(xn_sp), 0, [[TOK, 128], [128 * TOK, ECH], [1, TOK]])
                dma(sq, S["SH3"], XN3[:], src)
            block.sync(_x3)
            xn3_ld = S["SH3"].n

            upg_ld = []
            mm_up_pe = {}
            mm_ga_pe = {}
            silu_ac = {}
            mult_dv = {}
            hsp_sx = [0, 0]
            wd_ld_gp = [0] * FCH
            wd_st_sw = [0] * FCH

            def emit_upg_load(fc):
                def _w(gq, fc=fc):
                    if fc >= 2:
                        gq.wait_ge(PE.h, mm_ga_pe[(fc - 2, TG3 - 1)])
                    for wsrc, wdst in ((wupT, WUPB[fc % 2]), (wgT, WGTB[fc % 2])):
                        src = bass.AP(_th(wsrc), fc * 128,
                                      [[DFF, 128], [128 * DFF, ECH], [1, 128]])
                        upg_ld.append(dma(gq, S[f"GUP{fc % 2}"], wdst[:], src))
                block.gpsimd(_w)

            emit_upg_load(0)
            emit_upg_load(1)

            for fc in range(FCH):
                ld_cnt = upg_ld[fc * 2 + 1]
                for tg in range(TG3):
                    def _mm(t, fc=fc, tg=tg, ld_cnt=ld_cnt):
                        t.wait_ge(S[f"GUP{fc % 2}"].h, ld_cnt)
                        if fc == 0 and tg == 0:
                            t.wait_ge(S["SH3"].h, xn3_ld)
                        g2 = fc * TG3 + tg - 2
                        key2 = (g2 // TG3, g2 % TG3) if g2 >= 0 else None
                        if key2 in mult_dv:
                            t.wait_ge(DV.h, mult_dv[key2])
                        if key2 in silu_ac:
                            t.wait_ge(ACS.h, silu_ac[key2])
                        inst = None
                        for ec in range(ECH):
                            rhs = XN3[:, ec * TOK + tg * 512: ec * TOK + tg * 512 + 512]
                            inst = t.matmul(pUP[tg % 2][:], WUPB[fc % 2][:, ec * 128:(ec + 1) * 128],
                                            rhs, start=(ec == 0), stop=(ec == ECH - 1))
                        mm_up_pe[(fc, tg)] = p_mark(inst)
                        inst = None
                        for ec in range(ECH):
                            rhs = XN3[:, ec * TOK + tg * 512: ec * TOK + tg * 512 + 512]
                            inst = t.matmul(pGA[tg % 2][:], WGTB[fc % 2][:, ec * 128:(ec + 1) * 128],
                                            rhs, start=(ec == 0), stop=(ec == ECH - 1))
                        mm_ga_pe[(fc, tg)] = p_mark(inst)
                    block.tensor(_mm)

                    def _si(s, fc=fc, tg=tg):
                        s.wait_ge(PE.h, mm_ga_pe[(fc, tg)])
                        g2 = fc * TG3 + tg - 2
                        key2 = (g2 // TG3, g2 % TG3) if g2 >= 0 else None
                        if key2 in mult_dv:
                            s.wait_ge(DV.h, mult_dv[key2])
                        silu_ac[(fc, tg)] = a_op(
                            s, lambda: s.activation(SG[tg % 2][:], pGA[tg % 2][:], AF.Silu))
                    block.scalar(_si)

                    def _ml(v, fc=fc, tg=tg):
                        v.wait_ge(PE.h, mm_up_pe[(fc, tg)])
                        v.wait_ge(ACS.h, silu_ac[(fc, tg)])
                        if tg == 0:
                            v.wait_ge(S[f"SH{fc % 2}"].h, hsp_sx[fc % 2])
                        dst = HCH[fc % 2][:, tg * 512:(tg + 1) * 512]
                        mult_dv[(fc, tg)] = d_op(
                            v, lambda: v.tensor_tensor(dst, pUP[tg % 2][:], SG[tg % 2][:], ALU.mult))
                    block.vector(_ml)

                def _hs(sq, fc=fc):
                    sq.wait_ge(DV.h, mult_dv[(fc, TG3 - 1)])
                    dst = bass.AP(_th(h_sp), fc * 128 * TOK, [[TOK, 128], [1, TOK]])
                    dma(sq, S[f"SH{fc % 2}"], dst, HCH[fc % 2][:])
                block.sync(_hs)
                hsp_sx[fc % 2] = S[f"SH{fc % 2}"].n

                if fc + 2 < FCH:
                    emit_upg_load(fc + 2)

                def _wdl(gq, fc=fc):
                    if fc >= 2:
                        gq.wait_ge(S[f"SWD{fc % 2}"].h, wd_st_sw[fc - 2])
                    src = bass.AP(_th(wdT), fc * 128 * EMB, [[EMB, 128], [1, EMB]])
                    wd_ld_gp[fc] = dma(gq, S[f"GWD{fc % 2}"], WDS[fc % 2][:], src)
                block.gpsimd(_wdl)

                def _wds(sq, fc=fc):
                    sq.wait_ge(S[f"GWD{fc % 2}"].h, wd_ld_gp[fc])
                    dst = bass.AP(_th(wd_bf), fc * 128 * EMB, [[EMB, 128], [1, EMB]])
                    wd_st_sw[fc] = dma(sq, S[f"SWD{fc % 2}"], dst, WDS[fc % 2][:])
                block.sync(_wds)

        barrier()

        with contextlib.ExitStack() as scC:
            eC = scC.enter_context
            H3B = eC(nc.sbuf_tensor("H3B", [128, FCH * 512], BF16))
            WDB = [eC(nc.sbuf_tensor(f"WDB{i}", [128, FCH * 128], BF16)) for i in range(3)]
            YB = [eC(nc.sbuf_tensor(f"YB{i}", [128, 512], F32)) for i in range(2)]
            pY = [eC(nc.psum_tensor(f"pY{i}", [128, 512], F32)) for i in range(2)]

            ymm_pe = {}
            yev_dv = {}
            ysp_sx = {}
            h3_ld = [0] * SUB
            wd_ld = {}

            def emit_wd_load(sub, dm):
                def _w(sq, sub=sub, dm=dm):
                    gi = sub * DMCH + dm
                    if gi >= 3:
                        pgi = gi - 3
                        sq.wait_ge(PE.h, ymm_pe[(pgi // DMCH, pgi % DMCH)])
                    src = bass.AP(_th(wd_bf), dm * 128,
                                  [[EMB, 128], [128 * EMB, FCH], [1, 128]])
                    wd_ld[(sub, dm)] = dma(sq, S[f"SWB{gi % 3}"], WDB[gi % 3][:], src)
                block.sync(_w)

            FH = 43
            for sub in range(SUB):
                def _h3(sq, sub=sub):
                    if sub >= 1:
                        sq.wait_ge(PE.h, ymm_pe[(sub - 1, DMCH - 1)])
                    srcA = bass.AP(_th(h_sp), sub * 512,
                                   [[TOK, 128], [128 * TOK, FH], [1, 512]])
                    ca = dma(sq, S["SH3"], H3B[:, 0:FH * 512], srcA)
                    srcB = bass.AP(_th(h_sp), FH * 128 * TOK + sub * 512,
                                   [[TOK, 128], [128 * TOK, FCH - FH], [1, 512]])
                    cb = dma(sq, S["SH3B"], H3B[:, FH * 512:], srcB)
                    h3_ld[sub] = (ca, cb)
                block.sync(_h3)
                if sub == 0:
                    emit_wd_load(0, 0)
                    emit_wd_load(0, 1)
                    emit_wd_load(0, 2)

                for dm in range(DMCH):
                    gi = sub * DMCH + dm

                    def _mm(t, sub=sub, dm=dm, gi=gi):
                        t.wait_ge(S[f"SWB{gi % 3}"].h, wd_ld[(sub, dm)])
                        t.wait_ge(S["SH3"].h, h3_ld[sub][0])
                        key2 = (sub, dm - 2) if dm >= 2 else (sub - 1, dm + DMCH - 2)
                        if key2 in yev_dv:
                            t.wait_ge(DV.h, yev_dv[key2])
                        inst = None
                        for fcc in range(FH):
                            inst = t.matmul(pY[gi % 2][:], WDB[gi % 3][:, fcc * 128:(fcc + 1) * 128],
                                            H3B[:, fcc * 512:(fcc + 1) * 512],
                                            start=(fcc == 0), stop=False)
                        t.wait_ge(S["SH3B"].h, h3_ld[sub][1])
                        for fcc in range(FH, FCH):
                            inst = t.matmul(pY[gi % 2][:], WDB[gi % 3][:, fcc * 128:(fcc + 1) * 128],
                                            H3B[:, fcc * 512:(fcc + 1) * 512],
                                            start=False, stop=(fcc == FCH - 1))
                        ymm_pe[(sub, dm)] = p_mark(inst)
                    block.tensor(_mm)
                    ngi = gi + 3
                    if ngi < SUB * DMCH:
                        emit_wd_load(ngi // DMCH, ngi % DMCH)

                    def _ev(v, sub=sub, dm=dm, gi=gi):
                        v.wait_ge(PE.h, ymm_pe[(sub, dm)])
                        key2 = (sub, dm - 2) if dm >= 2 else (sub - 1, dm + DMCH - 2)
                        if key2 in ysp_sx:
                            v.wait_ge(S[f"SY{(key2[0] * DMCH + key2[1]) % 2}"].h, ysp_sx[key2])
                        yev_dv[(sub, dm)] = d_op(
                            v, lambda: v.tensor_copy(YB[gi % 2][:], pY[gi % 2][:]))
                    block.vector(_ev)

                    def _ys(gq, sub=sub, dm=dm, gi=gi):
                        gq.wait_ge(DV.h, yev_dv[(sub, dm)])
                        dst = bass.AP(_th(y_sp), dm * 128 * TOK + sub * 512,
                                      [[TOK, 128], [1, 512]])
                        ysp_sx[(sub, dm)] = dma(gq, S[f"SY{gi % 2}"], dst, YB[gi % 2][:])
                    block.gpsimd(_ys)

        barrier()

        with contextlib.ExitStack() as scD:
            eD = scD.enter_context
            YL = [eD(nc.sbuf_tensor(f"YL{i}", [128, EMB], F32)) for i in range(2)]
            AL = [eD(nc.sbuf_tensor(f"AL{i}", [128, EMB], F32)) for i in range(2)]
            XL = [eD(nc.sbuf_tensor(f"XL{i}", [128, EMB], F32)) for i in range(2)]
            SFM = eD(nc.sbuf_tensor("SFM", [128, EMB], F32))
            OT = [eD(nc.sbuf_tensor(f"OT{i}", [128, EMB], F32)) for i in range(2)]
            pTC = [eD(nc.psum_tensor(f"pTC{i}", [128, 512], F32)) for i in range(2)]

            add_dv = [0] * (TOK // 128)
            trc_pe = {}
            oadd_dv = {}
            out_g = [0, 0]
            sfm_tr_pe = 0

            for blk in range(TOK // 128):
                def _ld(sq, blk=blk):
                    if blk >= 2:
                        sq.wait_ge(DV.h, add_dv[blk - 2])
                    ysrc = bass.AP(_th(y_sp), blk * 128,
                                   [[TOK, 128], [128 * TOK, ECH], [1, 128]])
                    dma(sq, S[f"SCL{blk % 2}"], YL[blk % 2][:], ysrc)
                block.sync(_ld)

                def _ldg(gq, blk=blk):
                    if blk >= 2:
                        gq.wait_ge(DV.h, add_dv[blk - 2])
                        gq.wait_ge(S[f"GOUT{blk % 2}"].h, out_g[blk % 2])
                    asrc = bass.AP(_th(attn_sp), blk * 128,
                                   [[TOK, 128], [128 * TOK, ECH], [1, 128]])
                    dma(gq, S[f"GCL{blk % 2}"], AL[blk % 2][:], asrc)
                    xsrc = bass.AP(_th(x_in), blk * 128 * EMB, [[EMB, 128], [1, EMB]])
                    dma(gq, S[f"GCL{blk % 2}"], XL[blk % 2][:], xsrc)
                block.gpsimd(_ldg)
                here = S[f"SCL{blk % 2}"].n
                hereg = S[f"GCL{blk % 2}"].n

                def _ad(v, blk=blk, here=here, hereg=hereg, sfm_tr_pe=sfm_tr_pe):
                    v.wait_ge(S[f"SCL{blk % 2}"].h, here)
                    v.wait_ge(S[f"GCL{blk % 2}"].h, hereg)
                    if blk >= 1:
                        v.wait_ge(PE.h, sfm_tr_pe)
                    add_dv[blk] = d_op(
                        v, lambda: v.tensor_tensor(SFM[:], YL[blk % 2][:], AL[blk % 2][:], ALU.add))
                block.vector(_ad)

                for q2 in range(16):
                    def _tp(t, blk=blk, q2=q2):
                        t.wait_ge(DV.h, add_dv[blk])
                        for k in range(2):
                            c = q2 * 2 + k
                            n = blk * 32 + c
                            bank = pTC[n % 2]
                            slot = (n // 2) % 4
                            if n >= 2:
                                t.wait_ge(DV.h, oadd_dv.get(n - 2, 0))
                            inst = t.transpose(bank[:, slot * 128:(slot + 1) * 128],
                                               SFM[:, c * 128:(c + 1) * 128], idf[:])
                            trc_pe[n] = p_mark(inst)
                    block.tensor(_tp)

                    def _oa(v, blk=blk, q2=q2):
                        for k in range(2):
                            c = q2 * 2 + k
                            n = blk * 32 + c
                            if c == 0:
                                v.wait_ge(S[f"GOUT{blk % 2}"].h, out_g[blk % 2])
                            v.wait_ge(PE.h, trc_pe[n])
                            bank = pTC[n % 2]
                            slot = (n // 2) % 4
                            view = bank[:, slot * 128:(slot + 1) * 128]
                            dst = OT[blk % 2][:, c * 128:(c + 1) * 128]
                            xsl = XL[blk % 2][:, c * 128:(c + 1) * 128]
                            oadd_dv[n] = d_op(
                                v, lambda view=view, dst=dst, xsl=xsl:
                                v.tensor_tensor(dst, view, xsl, ALU.add))
                    block.vector(_oa)
                sfm_tr_pe = PE.n

                def _ow(gq, blk=blk):
                    gq.wait_ge(DV.h, oadd_dv[blk * 32 + 31])
                    dst = bass.AP(_th(out_h), blk * 128 * EMB, [[EMB, 128], [1, EMB]])
                    dma(gq, S[f"GOUT{blk % 2}"], dst, OT[blk % 2][:])
                block.gpsimd(_ow)
                out_g[blk % 2] = S[f"GOUT{blk % 2}"].n

            def _fin(gq):
                gq.wait_ge(S["GOUT0"].h, S["GOUT0"].n)
                gq.wait_ge(S["GOUT1"].h, S["GOUT1"].n)
            block.gpsimd(_fin)

    return nc


def kernel(L0_attn_x, L0_attn_Wq, L0_attn_bq, L0_attn_Wk, L0_attn_bk,
           L0_attn_Wv, L0_attn_bv, L0_ffn_W_gate, L0_ffn_W_up, L0_ffn_W_down):
    x = np.asarray(L0_attn_x, dtype=np.float32).reshape(16384, 4096)
    wq = np.asarray(L0_attn_Wq, dtype=np.float32)
    wk = np.asarray(L0_attn_Wk, dtype=np.float32)
    wv = np.asarray(L0_attn_Wv, dtype=np.float32)
    shared = {
        "wq_r": np.ascontiguousarray(np.transpose(wq, (0, 2, 1)).reshape(4096, 4096)),
        "wk_r": np.ascontiguousarray(np.transpose(wk, (0, 2, 1)).reshape(4096, 1024)),
        "wv_r": np.ascontiguousarray(np.transpose(wv, (0, 2, 1)).reshape(4096, 1024)),
        "bq": np.ascontiguousarray(np.asarray(L0_attn_bq, dtype=np.float32)),
        "bk": np.ascontiguousarray(np.asarray(L0_attn_bk, dtype=np.float32)),
        "bv": np.ascontiguousarray(np.asarray(L0_attn_bv, dtype=np.float32)),
        "wupT": np.ascontiguousarray(np.asarray(L0_ffn_W_up, dtype=np.float32).T),
        "wgT": np.ascontiguousarray(np.asarray(L0_ffn_W_gate, dtype=np.float32).T),
        "wdT": np.ascontiguousarray(np.asarray(L0_ffn_W_down, dtype=np.float32).T),
        "idn": np.eye(128, dtype=np.float32),
    }
    nc = build_nc()
    in_maps = []
    for c in range(8):
        m = dict(shared)
        m["x"] = np.ascontiguousarray(x[c * TOK:(c + 1) * TOK])
        in_maps.append(m)
    res = run_bass_kernel_spmd(nc, in_maps, core_ids=list(range(8)))
    out = np.concatenate([r["out"] for r in res.results], axis=0)
    return out.reshape(1, 16384, 4096)
